# version 01
# baseline (speedup 1.0000x reference)
import numpy as np
import ml_dtypes
import concourse.bass as bass
import concourse.mybir as mybir
from concourse.bass_utils import run_bass_kernel_spmd
from concourse.alu_op_type import AluOpType
from contextlib import ExitStack

F32 = mybir.dt.float32
BF16 = mybir.dt.bfloat16
AF = mybir.ActivationFunctionType
AX = mybir.AxisListType
MUL = AluOpType.mult
ADD = AluOpType.add
SUB = AluOpType.subtract
MAXOP = AluOpType.max
MINOP = AluOpType.min
BYP = AluOpType.bypass

T = 4096
D = 1024
DBG_L3 = False
TC = 512
NTC = T // TC
NEG = -1.0e30

ENGINES = ("sync", "tensor", "vector", "scalar", "gpsimd")
STRICT_SAME_ENGINE = True



class Ev:
    __slots__ = ("eng", "idx", "is_dma", "gseq")
    GSEQ = 0

    def __init__(self, eng, idx, is_dma):
        self.eng = eng
        self.idx = idx
        self.is_dma = is_dma
        Ev.GSEQ += 1
        self.gseq = Ev.GSEQ


class Prog:

    def __init__(self, nc, stack):
        self.nc = nc
        self.stack = stack
        self.ops = {e: [] for e in ENGINES}
        self.writer = {}
        self.readers = {}
        self.sem = {e: stack.enter_context(nc.semaphore("sem_" + e)) for e in ENGINES}
        self.final_evs = []

    def op(self, eng, fn, reads=(), writes=(), extra=(), is_dma=False):
        waits = []
        for b in reads:
            w = self.writer.get(b)
            if w is not None:
                waits.append(w)
        for b in writes:
            w = self.writer.get(b)
            if w is not None:
                waits.append(w)
            waits.extend(self.readers.get(b, ()))
        waits.extend(x for x in extra if x is not None)
        ev = Ev(eng, len(self.ops[eng]), is_dma)
        self.ops[eng].append((fn, waits, ev))
        for b in reads:
            self.readers.setdefault(b, []).append(ev)
        for b in writes:
            self.writer[b] = ev
            self.readers[b] = []
        return ev

    def dma(self, eng, out, in_, reads=(), writes=(), extra=()):
        return self.op(eng, lambda q: q.dma_start(out=out, in_=in_),
                       reads=reads, writes=writes, extra=extra, is_dma=True)

    def finalize(self, evs):
        self.final_evs.extend(evs)

    def emit(self, name="prog"):
        consumed = set()
        for e in ENGINES:
            for fn, waits, ev in self.ops[e]:
                if ev.is_dma:
                    consumed.add((e, ev.idx))
                for w in waits:
                    if STRICT_SAME_ENGINE or (w.eng != ev.eng or w.is_dma):
                        consumed.add((w.eng, w.idx))
        for w in self.final_evs:
            consumed.add((w.eng, w.idx))
        val = {}
        for e in ENGINES:
            c = 0
            for fn, waits, ev in self.ops[e]:
                if (e, ev.idx) in consumed:
                    c += 16 if ev.is_dma else 1
                    val[(e, ev.idx)] = (c, 16 if ev.is_dma else 1)
                else:
                    val[(e, ev.idx)] = (c, 0)
        nc = self.nc
        sem = self.sem
        ops = self.ops

        import bisect
        dma_gseq = {e: [] for e in ENGINES}
        dma_ext = {e: [] for e in ENGINES}
        for e in ENGINES:
            ops_e = self.ops[e]
            n = len(ops_e)
            run_end = [0] * n
            j = n - 1
            for i in range(n - 1, -1, -1):
                if i == n - 1:
                    run_end[i] = i
                elif not ops_e[i + 1][1]:
                    run_end[i] = run_end[i + 1]
                else:
                    run_end[i] = i
            for fn, waits, ev in ops_e:
                if ev.is_dma:
                    dma_gseq[e].append(ev.gseq)
                    re_ = run_end[ev.idx]
                    dma_ext[e].append(val[(e, re_)][0])

        def dma_target(qeng, j_val, waiter_gseq):
            i = bisect.bisect_left(dma_gseq[qeng], waiter_gseq)
            if i == 0:
                return j_val
            return max(j_val, dma_ext[qeng][i - 1])

        def body(engine_obj, e):
            lastwait = {}
            for fn, waits, ev in ops[e]:
                tgt = {}
                for w in waits:
                    if (not STRICT_SAME_ENGINE) and w.eng == e and not w.is_dma:
                        continue
                    v, inc = val[(w.eng, w.idx)]
                    if inc == 0:
                        raise RuntimeError("wait on unconsumed event")
                    if w.is_dma:
                        v = dma_target(w.eng, v, ev.gseq)
                    if v > tgt.get(w.eng, -1):
                        tgt[w.eng] = v
                for weng, v in sorted(tgt.items()):
                    if v > lastwait.get(weng, -1):
                        engine_obj.wait_ge(sem[weng], v)
                        lastwait[weng] = v
                inst = fn(engine_obj)
                v, inc = val[(e, ev.idx)]
                if inc:
                    if inst is None:
                        raise RuntimeError(f"op on {e} returned no instruction but needs inc")
                    inst.then_inc(sem[e], inc)
            tgt = {}
            for w in self.final_evs:
                v, inc = val[(w.eng, w.idx)]
                if v > tgt.get(w.eng, -1):
                    tgt[w.eng] = v
            if e == "sync":
                for weng, v in sorted(tgt.items()):
                    if v > lastwait.get(weng, -1):
                        engine_obj.wait_ge(sem[weng], v)

        with nc.Block(name, no_gpsimd_drain=True) as blk:
            for e in ENGINES:
                if not self.ops[e] and e != "sync":
                    continue
                getattr(blk, e)(lambda engine_obj, e=e: body(engine_obj, e))



def _alibi_slopes(heads):
    import math
    start = 2.0 ** (-2.0 ** (-(math.log2(heads) - 3)))
    return np.array([start * start ** i for i in range(heads)], dtype=np.float32)


def _colpack(a):
    a = np.asarray(a).reshape(-1)
    F = a.shape[0]
    K = (F + 127) // 128
    out = np.zeros((K, 128), dtype=np.float32)
    out.reshape(-1)[:F] = a
    return np.ascontiguousarray(out.T)


def host_prep(inp):
    f32 = lambda a: np.ascontiguousarray(a, dtype=np.float32)
    bf = lambda a: np.ascontiguousarray(np.asarray(a, dtype=np.float32).astype(ml_dtypes.bfloat16))
    h = {}
    h["c_ident_bf"] = np.eye(128, dtype=ml_dtypes.bfloat16)
    h["c_ident_f32"] = np.eye(128, dtype=np.float32)
    k = np.arange(D)
    ang = 2.0 * np.pi * np.outer(k, k) / D
    h["c_F_re"] = bf(np.cos(ang))
    h["c_F_im_neg"] = bf(np.sin(ang))
    h["c_G_re"] = bf(np.cos(ang) / D)
    h["c_G_im"] = bf(np.sin(ang) / D)
    h["c_G_im_neg"] = bf(-np.sin(ang) / D)
    p = np.arange(128)[:, None]
    f = np.arange(TC)[None, :]
    for rel in (0, 128, 256, 384):
        h[f"c_mask{rel}"] = f32(np.where(p + rel > f, NEG, 0.0))
    hs = 64
    t_idx = np.arange(1, T + 1, dtype=np.float32)[:, None]
    i_idx = np.arange(hs // 2, dtype=np.float32)
    theta = (1.0 / 10000.0) ** (2.0 * i_idx / hs)
    ang2 = (t_idx * theta).T
    h["c_cossin"] = f32(np.concatenate([np.cos(ang2), np.cos(ang2),
                                        np.sin(ang2), np.sin(ang2)], 0))
    slopes = _alibi_slopes(16)
    nkt = T // 128
    tk = np.arange(T).reshape(nkt, 128).T
    h["c_alibi_pcol"] = f32(np.broadcast_to(slopes[:, None, None], (16, 128, nkt))
                            * (tk[None] + 2.0))
    h["c_alibi_null"] = f32(slopes[:, None] * np.array([0.0, 1.0])[None, :])
    h["c_alibi_qrow"] = f32(slopes[:, None] * (np.arange(T)[None, :] + 2.0) / 8.0)
    h["c_sel7"] = f32(np.eye(8, dtype=np.float32)[:, 7:8])
    pq = np.arange(128)
    h["c_maskdiag"] = f32(np.where(pq[:, None] > pq[None, :], NEG, 0.0))
    sel32 = np.zeros((64, 2), dtype=np.float32)
    sel32[:32, 0] = 1.0
    sel32[32:, 1] = 1.0
    h["c_sel32bf"] = np.ascontiguousarray(sel32.astype(ml_dtypes.bfloat16))
    h["c_sel7f"] = f32(np.eye(8, dtype=np.float32)[:, 7:8])

    h["w_l0_Win_T"] = f32(inp["L0_attn_W_in"].T)
    h["wc_l0_conv"] = np.ascontiguousarray(
        _colpack(inp["L0_attn_conv_w"][:, 0, :].T.reshape(-1)).reshape(128, 4, 16)
        .transpose(0, 2, 1).reshape(128, 64))
    cw = inp["L0_attn_conv_w"][:, 0, :]
    h["wc_l0_conv"] = f32(cw.reshape(16, 128, 4).transpose(1, 0, 2).reshape(128, 64))
    h["wc_l0_convb"] = _colpack(inp["L0_attn_conv_b"])
    h["w_l0_Wx_T"] = f32(inp["L0_attn_W_x"].T)
    h["w_l0_Wdt_T"] = f32(inp["L0_attn_W_dt"].T)
    h["wc_l0_bdt"] = _colpack(inp["L0_attn_b_dt"])
    h["wc_l0_Alog"] = f32(inp["L0_attn_A_log"][:, None])
    h["wc_l0_D"] = _colpack(inp["L0_attn_D_param"])
    h["w_l0_Wout_T"] = f32(inp["L0_attn_W_out"].T)
    h["wc_l0_lng"] = _colpack(inp["L0_attn_ln_g"])
    h["wc_l0_lnb"] = _colpack(inp["L0_attn_ln_b"])
    h["w_l0_w1_T"] = f32(inp["L0_ffn_w1_w"].T)
    h["wc_l0_w1s"] = _colpack(inp["L0_ffn_w1_s"])
    h["w_l0_w2_T"] = f32(inp["L0_ffn_w2_w"].T)
    h["wc_l0_w2s"] = _colpack(inp["L0_ffn_w2_s"])
    h["w_l0_h1_T"] = f32(inp["L0_ffn_h1_w"].T)
    h["wc_l0_h1s"] = _colpack(inp["L0_ffn_h1_s"])
    h["w_l0_h2_T"] = f32(inp["L0_ffn_h2_w"].T)
    h["wc_l0_h2s"] = _colpack(inp["L0_ffn_h2_s"])
    h["w_l0_w3_T"] = f32(inp["L0_ffn_w3_w"].T)
    h["wc_l0_w3s"] = _colpack(inp["L0_ffn_w3_s"])

    h["wc_l1_ng"] = _colpack(inp["L1_attn_norm_g"])
    h["wc_l1_nb"] = _colpack(inp["L1_attn_norm_b"])
    h["w_l1_Wq"] = f32(inp["L1_attn_Wq"])
    h["w_l1_Wkv"] = f32(inp["L1_attn_Wkv"])
    h["wc_l1_qs"] = f32(inp["L1_attn_q_scale"][:, None])
    h["wc_l1_ks"] = f32(inp["L1_attn_k_scale"][:, None])
    null = inp["L1_attn_null_kv"].reshape(16, 2, 2, 64)
    nk = np.transpose(null[:, :, 0, :], (0, 2, 1)).reshape(16 * 64, 2)
    h["w_l1_nk"] = f32(nk)
    h["w_l1_nvT"] = f32(null[:, :, 1, :].reshape(32, 64))
    h["wc_l1_alibinull"] = f32((slopes[:, None] * np.array([0.0, 1.0])[None, :]).T)
    h["w_l1_Wout"] = f32(inp["L1_attn_Wout"])
    h["w_l1_wreT"] = bf(inp["L1_ffn_w_freq_real"].T)
    h["w_l1_wimT"] = bf(inp["L1_ffn_w_freq_imag"].T)
    h["wc_l1_bias"] = _colpack(inp["L1_ffn_bias"])

    h["wc_l2_lnw"] = _colpack(inp["L2_attn_ln_w"])
    h["wc_l2_lnb"] = _colpack(inp["L2_attn_ln_b"])
    h["w_l2_inproj_T"] = f32(inp["L2_attn_in_proj_w"].T)
    cw2 = inp["L2_attn_conv_w"]
    h["wc_l2_conv"] = f32(cw2.reshape(18, 128, 4).transpose(1, 0, 2).reshape(128, 72))
    h["wc_l2_convb"] = _colpack(inp["L2_attn_conv_b"])
    h["wc_l2_Alog"] = f32(inp["L2_attn_A_log"][:, None])
    h["wc_l2_D"] = f32(inp["L2_attn_D"][None, :])
    h["wc_l2_dtb"] = f32(inp["L2_attn_dt_bias"][:, None])
    h["wc_l2_gamma"] = _colpack(inp["L2_attn_gamma"])
    h["w_l2_outproj_T"] = f32(inp["L2_attn_out_proj_w"].T)
    h["w_l2_Win"] = f32(inp["L2_ffn_Win"])
    h["wc_l2_bin"] = f32(inp["L2_ffn_bin_"][:, None])
    h["wc_l2_W1"] = f32(inp["L2_ffn_W1"])
    h["wc_l2_b1"] = f32(inp["L2_ffn_b1"])
    h["wc_l2_W2"] = f32(inp["L2_ffn_W2"])
    h["wc_l2_b2"] = f32(inp["L2_ffn_b2"][:, None])
    h["w_l2_Wout"] = f32(inp["L2_ffn_Wout"])
    h["wc_l2_bout"] = _colpack(inp["L2_ffn_bout"])

    perm = []
    for j in range(8):
        h0, h1 = 2 * j, 2 * j + 1
        perm += list(h0 * 64 + np.arange(0, 64, 2))
        perm += list(h1 * 64 + np.arange(0, 64, 2))
        perm += list(h0 * 64 + np.arange(1, 64, 2))
        perm += list(h1 * 64 + np.arange(1, 64, 2))
    perm = np.array(perm)
    h["w_l3_WqT"] = f32(inp["L3_attn_Wq"].T[:, perm])
    h["wc_l3_bq"] = _colpack(inp["L3_attn_bq"][perm])
    h["w_l3_WkT"] = f32(inp["L3_attn_Wk"].T[:, perm])
    h["wc_l3_bk"] = _colpack(inp["L3_attn_bk"][perm])
    h["w_l3_WvT"] = f32(inp["L3_attn_Wv"].T)
    h["wc_l3_bv"] = _colpack(inp["L3_attn_bv"])
    h["w_l3_WoT"] = f32(inp["L3_attn_Wo"].T)
    h["wc_l3_bo"] = _colpack(inp["L3_attn_bo"])
    h["w_l3_WrT"] = f32(inp["L3_ffn_Wr"].T)
    h["w_l3_W1"] = f32(inp["L3_ffn_W1"][7])
    h["wc_l3_b1"] = _colpack(inp["L3_ffn_b1"][7])
    h["w_l3_W2"] = f32(inp["L3_ffn_W2"][7])
    h["wc_l3_b2"] = _colpack(inp["L3_ffn_b2"][7])
    return h


def host_specs(h):
    return {k: (list(v.shape), BF16 if v.dtype == ml_dtypes.bfloat16 else F32)
            for k, v in h.items()}


UF_SLOTS = 40
UB_SLOTS = 44
WB_SLOTS = 16
ROW_SLOTS = 8


class Ctx:
    def __init__(self, nc, stack, specs, taps=()):
        self.nc = nc
        self.stack = stack
        self.d = {}
        self.dshape = {}
        self.sb = {}
        self.taps = set(taps)
        for name, (shape, dt) in specs.items():
            self.d[name] = nc.dram_tensor(name, shape, dt, kind="ExternalInput").ap()
            self.dshape[name] = shape
        self.d["x_in"] = nc.dram_tensor("x_in", [D, T], F32, kind="ExternalInput").ap()
        self.d["out"] = nc.dram_tensor("out", [T, D], F32, kind="ExternalOutput").ap()
        self.ps = [stack.enter_context(nc.psum_tensor(f"psf{i}", [128, 512], F32))
                   for i in range(6)]
        self.psb = stack.enter_context(nc.psum_tensor("psbf", [128, 1024], BF16))
        self.prog = Prog(nc, stack)
        self.UF = self.sbuf("UF", [128, UF_SLOTS * 512], F32)
        self.UB = self.sbuf("UB", [128, UB_SLOTS * 512], BF16)
        self.WB = self.sbuf("WB", [128, WB_SLOTS * 1024], BF16)
        self.ROW = self.sbuf("ROWP", [1, ROW_SLOTS * 512], F32)

    def scratch(self, name, shape, dt):
        if name not in self.d:
            kind = "ExternalOutput" if name in self.taps else "Internal"
            self.d[name] = self.nc.dram_tensor(name, shape, dt, kind=kind).ap()
            self.dshape[name] = shape
        return self.d[name]

    def sbuf(self, name, shape, dt):
        if name not in self.sb:
            self.sb[name] = self.stack.enter_context(
                self.nc.sbuf_tensor(name, shape, dt))
        return self.sb[name]


def uf(ctx, i, n=1):
    assert i + n <= UF_SLOTS
    return ctx.UF[:, i * 512:(i + n) * 512], [("UF", j) for j in range(i, i + n)]


def ub(ctx, i, n=1):
    assert i + n <= UB_SLOTS
    return ctx.UB[:, i * 512:(i + n) * 512], [("UB", j) for j in range(i, i + n)]


def wbs(ctx, i, n=1):
    assert i + n <= WB_SLOTS
    return ctx.WB[:, i * 1024:(i + n) * 1024], [("WB", j) for j in range(i, i + n)]


def rowslot(ctx, i, n=1):
    assert i + n <= ROW_SLOTS
    return ctx.ROW[0:1, i * 512:(i + n) * 512], [("ROW", j) for j in range(i, i + n)]


def bank(ctx, i):
    return ctx.ps[i], [("PS", i)]



def emit_setup(ctx):
    P = ctx.prog
    ctx.ones_bf = ctx.sbuf("ones_bf", [128, 1], BF16)
    ctx.ones_f32 = ctx.sbuf("ones_f32", [128, 1], F32)
    ctx.onesrow_f32 = ctx.sbuf("onesrow_f32", [1, 128], F32)
    ctx.onesrow_bf = ctx.sbuf("onesrow_bf", [1, 128], BF16)
    ctx.cpool = ctx.sbuf("cpool", [128, 8], F32)
    ctx.crow = ctx.sbuf("crow", [1, 8], F32)

    def mset(v):
        v.memset(ctx.ones_bf[:], 1.0)
        v.memset(ctx.ones_f32[:], 1.0)
        v.memset(ctx.onesrow_f32[:], 1.0)
        v.memset(ctx.onesrow_bf[:], 1.0)
        v.memset(ctx.cpool[:, 0:1], 0.0)
        v.memset(ctx.cpool[:, 1:2], 1.0)
        v.memset(ctx.crow[0:1, 0:1], 0.0)
        return v.memset(ctx.crow[0:1, 1:2], 1.0)
    P.op("vector", mset, writes=[("CONST", "base")])
    ctx.c_zero = ctx.cpool[:, 0:1]
    ctx.c_one = ctx.cpool[:, 1:2]
    ctx.CB = ("CONST", "base")

    ctx.ident_bf = ctx.sbuf("ident_bf", [128, 128], BF16)
    ctx.ident_f32 = ctx.sbuf("ident_f32", [128, 128], F32)
    P.dma("sync", ctx.ident_bf[:], ctx.d["c_ident_bf"], writes=[("CONST", "idb")])
    P.dma("sync", ctx.ident_f32[:], ctx.d["c_ident_f32"], writes=[("CONST", "idf")])
    ctx.masks = {}
    for rel in (0, 128, 256, 384):
        tile = ctx.sbuf(f"mask{rel}", [128, TC], F32)
        P.dma("sync", tile[:], ctx.d[f"c_mask{rel}"], writes=[("CONST", f"mask{rel}")])
        ctx.masks[rel] = tile
    ctx.cols = {}
    for name, shape in list(ctx.dshape.items()):
        if name.startswith("wc_"):
            tile = ctx.sbuf("sb_" + name, shape, F32)
            P.dma("sync", tile[:], ctx.d[name], writes=[("CONST", name)])
            ctx.cols[name] = tile


def col(ctx, name, k=0):
    t = ctx.cols["wc_" + name]
    return t[:, k:k + 1], ("CONST", "wc_" + name)



def emit_precast_all(ctx, names_fin_fout):
    P = ctx.prog
    for name, Fin, Fout in names_fin_fout:
        src = ctx.d[name]
        dst = ctx.scratch("wb_" + name, [Fin, Fout], BF16)
        KC = (Fin + 127) // 128
        CW = min(Fout, 2048)
        for k in range(KC):
            rows = min(128, Fin - k * 128)
            for c0 in range(0, Fout, CW):
                cw = min(CW, Fout - c0)
                fa, fak = uf(ctx, 32 + (c0 // CW + k) % 2 * 4, (cw + 511) // 512)
                ba, bak = ub(ctx, 36 + (c0 // CW + k) % 2 * 4, (cw + 511) // 512)
                P.dma("sync", fa[:rows, 0:cw], src[k * 128:k * 128 + rows, c0:c0 + cw],
                      writes=fak)
                P.op("vector", lambda v, d=ba[:rows, 0:cw], s=fa[:rows, 0:cw]:
                     v.tensor_copy(d, s), reads=fak, writes=bak)
                P.dma("sync", dst[k * 128:k * 128 + rows, c0:c0 + cw], ba[:rows, 0:cw],
                      reads=bak, writes=[("wb_" + name, 0)])



def emit_ln(ctx, name, x_dram, xkey, Fdim, out_bf_dram=None, obkey=None,
            g=None, b=None, out_f32_dram=None, ofkey=None, eps=1e-5,
            src_bf16=False):
    P = ctx.prog
    KC = Fdim // 128
    assert KC <= 16
    erow, erowk = rowslot(ctx, 6)
    P.op("vector", lambda v: v.memset(erow[0:1, 0:1], eps), writes=erowk)
    ek = erowk[0]
    for c in range(NTC):
        xbf, xbf_k = ub(ctx, 0, KC)
        xsq, xsq_k = ub(ctx, 16, KC)
        if not src_bf16:
            xt, xt_k = uf(ctx, 0, KC)
        else:
            xt, xt_k = xbf, xbf_k
        tmp, tmp_k = uf(ctx, 16, 1)
        outf, outf_k = uf(ctx, 17, 1)
        s1, s1k = bank(ctx, 0)
        s2, s2k = bank(ctx, 1)
        rbc, rbck = bank(ctx, 2)
        mbc, mbck = bank(ctx, 3)
        rm, rmk = rowslot(ctx, 0)
        rt, rtk = rowslot(ctx, 1)
        rv, rvk = rowslot(ctx, 2)
        rs, rsk = rowslot(ctx, 3)
        rr, rrk = rowslot(ctx, 4)
        rmr, rmrk = rowslot(ctx, 5)
        for k in range(KC):
            xs = xt[:, k * 512:(k + 1) * 512]
            bfk = xbf[:, k * 512:(k + 1) * 512]
            sqk = xsq[:, k * 512:(k + 1) * 512]
            P.dma("sync", xs, x_dram[k * 128:(k + 1) * 128, c * TC:(c + 1) * TC],
                  reads=[xkey], writes=[xt_k[k]])
            if not src_bf16:
                P.op("vector", lambda v, d=bfk, s=xs: v.tensor_copy(d, s),
                     reads=[xt_k[k]], writes=[xbf_k[k]])
            P.op("scalar", lambda s, d=sqk, si=xs:
                 s.activation(out=d, in_=si, func=AF.Square),
                 reads=[xt_k[k]], writes=[xsq_k[k]])
        for k in range(KC):
            P.op("tensor", lambda t, k=k, xbf=xbf: t.matmul(
                s1[0:1, :], ctx.ones_bf[:], xbf[:, k * 512:(k + 1) * 512],
                start=(k == 0), stop=(k == KC - 1)),
                reads=[xbf_k[k], ctx.CB], writes=s1k)
        for k in range(KC):
            P.op("tensor", lambda t, k=k, xsq=xsq: t.matmul(
                s2[0:1, :], ctx.ones_bf[:], xsq[:, k * 512:(k + 1) * 512],
                start=(k == 0), stop=(k == KC - 1)),
                reads=[xsq_k[k], ctx.CB], writes=s2k)
        P.op("vector", lambda v: v.tensor_scalar(
            out=rm, in0=s1[0:1, :], scalar1=1.0 / Fdim, scalar2=None, op0=MUL),
            reads=s1k, writes=rmk)
        P.op("vector", lambda v: v.tensor_tensor(out=rt, in0=rm, in1=rm, op=MUL),
             reads=rmk, writes=rtk)
        P.op("vector", lambda v: v.scalar_tensor_tensor(
            out=rv, in0=s2[0:1, :], scalar=1.0 / Fdim, in1=rt, op0=MUL, op1=SUB),
            reads=s2k + rtk, writes=rvk)
        P.op("scalar", lambda s: s.activation(out=rs, in_=rv, func=AF.Sqrt,
                                              bias=erow[0:1, 0:1]),
             reads=rvk + [ek], writes=rsk)
        P.op("vector", lambda v: v.reciprocal(rr, rs), reads=rsk, writes=rrk)
        P.op("vector", lambda v: v.tensor_tensor(out=rmr, in0=rm, in1=rr, op=MUL),
             reads=rmk + rrk, writes=rmrk)
        P.op("tensor", lambda t: t.matmul(rbc[:], ctx.onesrow_f32[:], rr,
                                          start=True, stop=True),
             reads=rrk + [ctx.CB], writes=rbck)
        P.op("tensor", lambda t: t.matmul(mbc[:], ctx.onesrow_f32[:], rmr,
                                          start=True, stop=True),
             reads=rmrk + [ctx.CB], writes=mbck)
        for k in range(KC):
            xs = xt[:, k * 512:(k + 1) * 512]
            src_k = xt_k[k]
            P.op("vector", lambda v, xs=xs: v.tensor_tensor(
                out=tmp, in0=xs, in1=rbc[:], op=MUL),
                reads=[src_k] + rbck, writes=tmp_k)
            need_f32 = (out_f32_dram is not None) or (g is not None)
            if need_f32:
                P.op("vector", lambda v: v.tensor_tensor(
                    out=outf, in0=tmp, in1=mbc[:], op=SUB),
                    reads=tmp_k + mbck, writes=outf_k)
                if out_f32_dram is not None:
                    P.dma("sync", out_f32_dram[k * 128:(k + 1) * 128,
                                               c * TC:(c + 1) * TC], outf,
                          reads=outf_k, writes=[(ofkey, 0)])
                if out_bf_dram is not None:
                    obt, obk = ub(ctx, 40, 1)
                    if g is not None:
                        gap, gk = col(ctx, g, k)
                        bap, bk_ = col(ctx, b, k)
                        P.op("vector", lambda v, gap=gap, bap=bap: v.tensor_scalar(
                            out=obt, in0=outf, scalar1=gap, scalar2=bap,
                            op0=MUL, op1=ADD),
                            reads=outf_k + [gk, bk_], writes=obk)
                    else:
                        P.op("scalar", lambda s: s.activation(
                            out=obt, in_=outf, func=AF.Identity),
                            reads=outf_k, writes=obk)
                    P.dma("sync", out_bf_dram[k * 128:(k + 1) * 128,
                                              c * TC:(c + 1) * TC], obt,
                          reads=obk, writes=[(obkey, 0)])
            else:
                obt, obk = ub(ctx, 40, 1)
                P.op("vector", lambda v: v.tensor_tensor(
                    out=obt, in0=tmp, in1=mbc[:], op=SUB),
                    reads=tmp_k + mbck, writes=obk)
                P.dma("sync", out_bf_dram[k * 128:(k + 1) * 128,
                                          c * TC:(c + 1) * TC], obt,
                      reads=obk, writes=[(obkey, 0)])



def emit_linear(ctx, name, src_dram, skey, Fin, Fout, w_dram, wkey, epi,
                Tlen=None, mtile_cap=None):
    Tlen = T if Tlen is None else Tlen
    """W bf16 [Fin, Fout] (lhsT orientation); src bf16 [Fin, Tlen].
    epi(c, mt_idx, m0, rows, bank_ap, bank_keys) -> records epilogue + stores.
    """
    P = ctx.prog
    KC = (Fin + 127) // 128
    klast = Fin - (KC - 1) * 128
    mtiles = []
    m0 = 0
    while m0 < Fout:
        rows = min(128, Fout - m0)
        mtiles.append((m0, rows))
        m0 += rows
    MB = max(1, (WB_SLOTS * 1024) // (KC * 128))
    if mtile_cap:
        MB = min(MB, mtile_cap)
    nbank = 0
    for g0 in range(0, len(mtiles), MB):
        group = mtiles[g0:g0 + MB]
        gw = sum(r for _, r in group)
        for k in range(KC):
            rk = 128 if k < KC - 1 else klast
            wslab = ctx.WB[:, k * MB * 128: k * MB * 128 + gw]
            P.dma("sync", wslab[0:rk, :],
                  w_dram[k * 128:k * 128 + rk, group[0][0]:group[0][0] + gw],
                  reads=[wkey], writes=[("WB", k)])
        for c in range(Tlen // TC):
            st, st_k = ub(ctx, 0, KC)
            for k in range(KC):
                rk = 128 if k < KC - 1 else klast
                P.dma("sync", st[0:rk, k * 512:(k + 1) * 512],
                      src_dram[k * 128:k * 128 + rk, c * TC:(c + 1) * TC],
                      reads=[skey], writes=[st_k[k]])
            off = 0
            for mi, (m0, rows) in enumerate(group):
                bk, bkk = bank(ctx, nbank % 4)
                nbank += 1
                for k in range(KC):
                    rk = 128 if k < KC - 1 else klast
                    lhs = ctx.WB[0:rk, k * MB * 128 + off: k * MB * 128 + off + rows]
                    rhs = st[0:rk, k * 512:(k + 1) * 512]
                    P.op("tensor", lambda t, lhs=lhs, rhs=rhs, k=k, bk=bk, rows=rows:
                         t.matmul(bk[0:rows, :], lhs, rhs,
                                  start=(k == 0), stop=(k == KC - 1)),
                        reads=[("WB", k), st_k[k]], writes=bkk)
                epi(c, g0 + mi, m0, rows, bk, bkk)
                off += rows


def emit_linear_dual(ctx, name, src_dram, skey, Fin, Fout, wa_dram, wakey,
                     wb_dram, wbkey, epi, Tlen=None):
    Tlen = T if Tlen is None else Tlen
    """Two matmuls sharing src (w1/w2 FFN): epi(c, mt, m0, rows, bankA, kA, bankB, kB)."""
    P = ctx.prog
    KC = Fin // 128
    MB = max(1, (WB_SLOTS * 1024) // (2 * KC * 128))
    mtiles = [(m0, 128) for m0 in range(0, Fout, 128)]
    nbank = 0
    for g0 in range(0, len(mtiles), MB):
        group = mtiles[g0:g0 + MB]
        gw = sum(r for _, r in group)
        for wi, (wd, wk) in enumerate(((wa_dram, wakey), (wb_dram, wbkey))):
            for k in range(KC):
                base = (wi * KC + k) * MB * 128
                wslab = ctx.WB[:, base: base + gw]
                P.dma("sync", wslab,
                      wd[k * 128:(k + 1) * 128, group[0][0]:group[0][0] + gw],
                      reads=[wk], writes=[("WB", wi * KC + k)])
        for c in range(Tlen // TC):
            st, st_k = ub(ctx, 0, KC)
            for k in range(KC):
                P.dma("sync", st[:, k * 512:(k + 1) * 512],
                      src_dram[k * 128:(k + 1) * 128, c * TC:(c + 1) * TC],
                      reads=[skey], writes=[st_k[k]])
            off = 0
            for mi, (m0, rows) in enumerate(group):
                bka, bkak = bank(ctx, nbank % 2 * 2)
                bkb, bkbk = bank(ctx, nbank % 2 * 2 + 1)
                nbank += 1
                for wi, (bk_, bkk_) in enumerate(((bka, bkak), (bkb, bkbk))):
                    for k in range(KC):
                        base = (wi * KC + k) * MB * 128
                        lhs = ctx.WB[:, base + off: base + off + rows]
                        rhs = st[:, k * 512:(k + 1) * 512]
                        P.op("tensor", lambda t, lhs=lhs, rhs=rhs, k=k, bk_=bk_, rows=rows:
                             t.matmul(bk_[0:rows, :], lhs, rhs,
                                      start=(k == 0), stop=(k == KC - 1)),
                             reads=[("WB", wi * KC + k), st_k[k]], writes=bkk_)
                epi(c, g0 + mi, m0, rows, bka, bkak, bkb, bkbk)
                off += rows


def emit_add3(ctx, name, a_dram, akey, b_dram, bkey, c_dram, ckey,
              out_dram, okey, Fdim=D):
    P = ctx.prog
    KC = Fdim // 128
    for c in range(NTC):
        for k in range(KC):
            ta, tak = uf(ctx, 32, 1)
            tb, tbk = uf(ctx, 33, 1)
            tc_, tck = uf(ctx, 34, 1)
            t1, t1k = uf(ctx, 35, 1)
            P.dma("sync", ta, a_dram[k * 128:(k + 1) * 128, c * TC:(c + 1) * TC],
                  reads=[akey], writes=tak)
            P.dma("sync", tb, b_dram[k * 128:(k + 1) * 128, c * TC:(c + 1) * TC],
                  reads=[bkey], writes=tbk)
            P.dma("sync", tc_, c_dram[k * 128:(k + 1) * 128, c * TC:(c + 1) * TC],
                  reads=[ckey], writes=tck)
            P.op("vector", lambda v: v.tensor_tensor(out=t1, in0=ta, in1=tb, op=ADD),
                 reads=tak + tbk, writes=t1k)
            P.op("vector", lambda v: v.tensor_tensor(out=ta, in0=t1, in1=tc_, op=ADD),
                 reads=t1k + tck, writes=tak)
            P.dma("sync", out_dram[k * 128:(k + 1) * 128, c * TC:(c + 1) * TC], ta,
                  reads=tak, writes=[(okey, 0)])



def emit_l0(ctx, x_dram, xkey):
    P = ctx.prog
    nc = ctx.nc
    xn0 = ctx.scratch("xn0", [D, T], BF16)
    xp = ctx.scratch("l0_xp", [2048, T], F32)
    sz0 = ctx.scratch("l0_sz", [2048, T], F32)
    xc = ctx.scratch("l0_xc", [2048, T], F32)
    xcbf = ctx.scratch("l0_xcbf", [2048, T], BF16)
    dt0 = ctx.scratch("l0_dt", [2048, T], F32)
    w0 = ctx.scratch("l0_w", [2048, T], F32)
    bcbf = ctx.scratch("l0_bcbf", [32, T], BF16)
    y0 = ctx.scratch("l0_y", [2048, T], F32)
    ya = ctx.scratch("l0_ya", [2048, T], BF16)
    att0 = ctx.scratch("l0_att", [D, T], F32)
    hh = ctx.scratch("l0_hh", [4096, T], BF16)
    g1 = ctx.scratch("l0_g1", [4096, T], BF16)
    g2 = ctx.scratch("l0_g2", [4096, T], BF16)
    ffn0 = ctx.scratch("l0_ffn", [D, T], F32)

    emit_ln(ctx, "ln0", x_dram, xkey, D, out_bf_dram=xn0, obkey="xn0")

    def epi_xz(c, mt, m0, rows, bk, bkk):
        o, ok = uf(ctx, 32 + mt % 2, 1)
        if m0 < 2048:
            P.op("vector", lambda v: v.tensor_copy(o, bk[:]), reads=bkk, writes=ok)
            P.dma("sync", xp[m0:m0 + 128, c * TC:(c + 1) * TC], o,
                  reads=ok, writes=[("l0_xp", 0)])
        else:
            P.op("scalar", lambda s: s.activation(out=o, in_=bk[:], func=AF.Silu),
                 reads=bkk, writes=ok)
            P.dma("sync", sz0[m0 - 2048:m0 - 2048 + 128, c * TC:(c + 1) * TC], o,
                  reads=ok, writes=[("l0_sz", 0)])
    emit_linear(ctx, "l0xz", xn0, ("xn0", 0), D, 4096,
                ctx.d["wb_w_l0_Win_T"], ("wb_w_l0_Win_T", 0), epi_xz)

    for cc in range(16):
        for c in range(NTC):
            xin, xink = uf(ctx, 32, 2)
            acc_a, acc_ak = uf(ctx, 34, 1)
            acc_b, acc_bk = uf(ctx, 35, 1)
            of, ofk = uf(ctx, 36, 1)
            obf, obk = ub(ctx, 40, 1)
            halo = xin[:, 0:515]
            if c == 0:
                P.op("vector", lambda v: v.memset(xin[:, 0:3], 0.0), writes=xink)
                P.dma("sync", xin[:, 3:515],
                      xp[cc * 128:(cc + 1) * 128, 0:512],
                      reads=[("l0_xp", 0)], writes=xink)
            else:
                P.dma("sync", xin[:, 0:515],
                      xp[cc * 128:(cc + 1) * 128, c * TC - 3:(c + 1) * TC],
                      reads=[("l0_xp", 0)], writes=xink)
            cw, cwk = col(ctx, "l0_conv", 0)
            convcols = ctx.cols["wc_l0_conv"]
            P.op("vector", lambda v, cc=cc: v.tensor_scalar(
                out=acc_a, in0=halo[:, 0:512], scalar1=convcols[:, cc * 4:cc * 4 + 1],
                scalar2=None, op0=MUL),
                reads=xink + [cwk], writes=acc_ak)
            P.op("vector", lambda v, cc=cc: v.scalar_tensor_tensor(
                out=acc_b, in0=halo[:, 1:513], scalar=convcols[:, cc * 4 + 1:cc * 4 + 2],
                in1=acc_a, op0=MUL, op1=ADD),
                reads=xink + acc_ak + [cwk], writes=acc_bk)
            P.op("vector", lambda v, cc=cc: v.scalar_tensor_tensor(
                out=acc_a, in0=halo[:, 2:514], scalar=convcols[:, cc * 4 + 2:cc * 4 + 3],
                in1=acc_b, op0=MUL, op1=ADD),
                reads=xink + acc_bk + [cwk], writes=acc_ak)
            P.op("vector", lambda v, cc=cc: v.scalar_tensor_tensor(
                out=acc_b, in0=halo[:, 3:515], scalar=convcols[:, cc * 4 + 3:cc * 4 + 4],
                in1=acc_a, op0=MUL, op1=ADD),
                reads=xink + acc_ak + [cwk], writes=acc_bk)
            bcol, bcolk = col(ctx, "l0_convb", cc)
            P.op("scalar", lambda s, bcol=bcol: s.activation(
                out=of, in_=acc_b, func=AF.Silu, bias=bcol),
                reads=acc_bk + [bcolk], writes=ofk)
            P.dma("sync", xc[cc * 128:(cc + 1) * 128, c * TC:(c + 1) * TC], of,
                  reads=ofk, writes=[("l0_xc", 0)])
            P.op("vector", lambda v: v.tensor_copy(obf, of), reads=ofk, writes=obk)
            P.dma("sync", xcbf[cc * 128:(cc + 1) * 128, c * TC:(c + 1) * TC], obf,
                  reads=obk, writes=[("l0_xcbf", 0)])

    def epi_bc(c, mt, m0, rows, bk, bkk):
        o, ok = ub(ctx, 41, 1)
        P.op("vector", lambda v: v.tensor_copy(o[0:32, :], bk[0:32, :]),
             reads=bkk, writes=ok)
        P.dma("sync", bcbf[0:32, c * TC:(c + 1) * TC], o[0:32, :],
              reads=ok, writes=[("l0_bcbf", 0)])
    emit_linear(ctx, "l0bc", xcbf, ("l0_xcbf", 0), 2048, 32,
                ctx.d["wb_w_l0_Wx_T"], ("wb_w_l0_Wx_T", 0), epi_bc)

    def epi_dt(c, mt, m0, rows, bk, bkk):
        ua, uak = uf(ctx, 32, 1)
        e1, e1k = uf(ctx, 33, 1)
        l1_, l1k = uf(ctx, 34, 1)
        r1, r1k = uf(ctx, 35, 1)
        dtt, dttk = uf(ctx, 36, 1)
        xct, xctk = uf(ctx, 37, 1)
        wt, wtk = uf(ctx, 38, 1)
        bcol, bcolk = col(ctx, "l0_bdt", mt)
        P.op("scalar", lambda s, bcol=bcol: s.activation(
            out=ua, in_=bk[:], func=AF.Abs, bias=bcol),
            reads=bkk + [bcolk], writes=uak)
        P.op("scalar", lambda s: s.activation(out=e1, in_=ua, func=AF.Exp, scale=-1.0),
             reads=uak, writes=e1k)
        P.op("scalar", lambda s: s.activation(out=l1_, in_=e1, func=AF.Ln,
                                              bias=ctx.crow[0:1, 1:2] if False else ctx.c_one),
             reads=e1k + [ctx.CB], writes=l1k)
        P.op("scalar", lambda s, bcol=bcol: s.activation(
            out=r1, in_=bk[:], func=AF.Relu, bias=bcol),
            reads=bkk + [bcolk], writes=r1k)
        P.op("vector", lambda v: v.tensor_tensor(out=dtt, in0=l1_, in1=r1, op=ADD),
             reads=l1k + r1k, writes=dttk)
        P.dma("sync", dt0[m0:m0 + 128, c * TC:(c + 1) * TC], dtt,
              reads=dttk, writes=[("l0_dt", 0)])
        P.dma("sync", xct, xc[m0:m0 + 128, c * TC:(c + 1) * TC],
              reads=[("l0_xc", 0)], writes=xctk)
        P.op("vector", lambda v: v.tensor_tensor(out=wt, in0=dtt, in1=xct, op=MUL),
             reads=dttk + xctk, writes=wtk)
        P.dma("sync", w0[m0:m0 + 128, c * TC:(c + 1) * TC], wt,
              reads=wtk, writes=[("l0_w", 0)])
    emit_linear(ctx, "l0dt", xcbf, ("l0_xcbf", 0), 2048, 2048,
                ctx.d["wb_w_l0_Wdt_T"], ("wb_w_l0_Wdt_T", 0), epi_dt)

    arow_ps, arow_psk = bank(ctx, 4)
    abc_ps, abc_psk = bank(ctx, 5)
    arow, arowk = rowslot(ctx, 7)
    A_bc = ctx.sbuf("l0_Abc", [128, 16], F32)
    alog = ctx.cols["wc_l0_Alog"]
    P.op("tensor", lambda t: t.transpose(arow_ps[0:1, 0:16], alog[0:16, 0:1],
                                         ctx.ident_f32[0:16, 0:16]),
         reads=[("CONST", "wc_l0_Alog"), ("CONST", "idf")], writes=arow_psk)
    P.op("scalar", lambda s: s.activation(out=arow[0:1, 0:16], in_=arow_ps[0:1, 0:16],
                                          func=AF.Exp),
         reads=arow_psk, writes=arowk)
    P.op("vector", lambda v: v.tensor_scalar(out=arow[0:1, 16:32],
                                             in0=arow[0:1, 0:16], scalar1=-1.0,
                                             scalar2=None, op0=MUL),
         reads=arowk, writes=arowk)
    P.op("tensor", lambda t: t.matmul(abc_ps[:, 0:16], ctx.onesrow_f32[:],
                                      arow[0:1, 16:32], start=True, stop=True),
         reads=arowk + [ctx.CB], writes=abc_psk)
    P.op("vector", lambda v: v.tensor_copy(A_bc[:], abc_ps[:, 0:16]),
         reads=abc_psk, writes=[("l0_Abc", 0)])

    HT = T // 2
    SL = T // 512
    SH = SL // 2
    for dtile in range(16):
        dtv, dtvk = uf(ctx, 0, SL)
        wv, wvk = uf(ctx, 8, SL)
        P.dma("sync", dtv, dt0[dtile * 128:(dtile + 1) * 128, :],
              reads=[("l0_dt", 0)], writes=dtvk)
        P.dma("sync", wv, w0[dtile * 128:(dtile + 1) * 128, :],
              reads=[("l0_w", 0)], writes=wvk)
        yv = [uf(ctx, 16, SH), uf(ctx, 20, SH)]
        hv = [uf(ctx, 32, SH), uf(ctx, 36, SH)]
        for s in range(16):
            Bb, Bbk = ub(ctx, 0, SL)
            Cb, Cbk = ub(ctx, 8, SL)
            brow_bc = bass.AP(ctx.d["l0_bcbf"].tensor, s * T, [[0, 128], [1, T]])
            crow_bc = bass.AP(ctx.d["l0_bcbf"].tensor, (16 + s) * T, [[0, 128], [1, T]])
            P.dma("sync", Bb, brow_bc, reads=[("l0_bcbf", 0)], writes=Bbk)
            P.dma("sync", Cb, crow_bc, reads=[("l0_bcbf", 0)], writes=Cbk)
            for half in range(2):
                av, avk = uf(ctx, 24, SH)
                bv, bvk = uf(ctx, 28, SH)
                h, hk = hv[half]
                y, yk = yv[half]
                dth = dtv[:, half * HT:(half + 1) * HT]
                wh = wv[:, half * HT:(half + 1) * HT]
                Bh = Bb[:, half * HT:(half + 1) * HT]
                Ch = Cb[:, half * HT:(half + 1) * HT]
                P.op("scalar", lambda se, s=s, av=av, dth=dth: se.activation(
                    out=av, in_=dth, func=AF.Exp, scale=A_bc[:, s:s + 1]),
                    reads=dtvk + [("l0_Abc", 0)], writes=avk)
                P.op("gpsimd", lambda g, bv=bv, wh=wh, Bh=Bh: g.tensor_tensor(
                    out=bv, in0=wh, in1=Bh, op=MUL),
                    reads=wvk + Bbk, writes=bvk)
                if half == 0:
                    P.op("vector", lambda v, h=h, av=av, bv=bv: v.tensor_tensor_scan(
                        out=h, data0=av, data1=bv, initial=0.0, op0=MUL, op1=ADD),
                        reads=avk + bvk, writes=hk)
                else:
                    hprev = hv[0][0][:, HT - 1:HT]
                    P.op("vector", lambda v, h=h, av=av, bv=bv, hprev=hprev:
                         v.tensor_tensor_scan(
                        out=h, data0=av, data1=bv, initial=hprev, op0=MUL, op1=ADD),
                        reads=avk + bvk + hv[0][1], writes=hk)
                P.op("gpsimd", lambda g, bv=bv, h=h, Ch=Ch: g.tensor_tensor(
                    out=bv, in0=h, in1=Ch, op=MUL),
                    reads=hk + Cbk + bvk, writes=bvk)
                if s == 0:
                    P.op("vector", lambda v, y=y, bv=bv: v.tensor_copy(y, bv),
                         reads=bvk, writes=yk)
                else:
                    P.op("vector", lambda v, y=y, bv=bv: v.tensor_tensor(
                        out=y, in0=y, in1=bv, op=ADD),
                        reads=bvk + yk, writes=yk)
        P.dma("sync", y0[dtile * 128:(dtile + 1) * 128, 0:HT], yv[0][0],
              reads=yv[0][1], writes=[("l0_y", 0)])
        P.dma("sync", y0[dtile * 128:(dtile + 1) * 128, HT:T], yv[1][0],
              reads=yv[1][1], writes=[("l0_y", 0)])

    KC = 16
    erow2, erow2k = rowslot(ctx, 6)
    for c in range(NTC):
        yt, yt_k = uf(ctx, 0, KC)
        ybf, ybf_k = ub(ctx, 0, KC)
        ysq, ysq_k = ub(ctx, 16, KC)
        s1, s1k = bank(ctx, 0)
        s2, s2k = bank(ctx, 1)
        rbc, rbck = bank(ctx, 2)
        mbc, mbck = bank(ctx, 3)
        rm, rmk = rowslot(ctx, 0)
        rt, rtk = rowslot(ctx, 1)
        rv, rvk = rowslot(ctx, 2)
        rs, rsk = rowslot(ctx, 3)
        rr, rrk = rowslot(ctx, 4)
        rmr, rmrk = rowslot(ctx, 5)
        for k in range(KC):
            ys = yt[:, k * 512:(k + 1) * 512]
            P.dma("sync", ys, y0[k * 128:(k + 1) * 128, c * TC:(c + 1) * TC],
                  reads=[("l0_y", 0)], writes=[yt_k[k]])
            P.op("vector", lambda v, d=ybf[:, k * 512:(k + 1) * 512], s=ys:
                 v.tensor_copy(d, s), reads=[yt_k[k]], writes=[ybf_k[k]])
            P.op("scalar", lambda s, d=ysq[:, k * 512:(k + 1) * 512], si=ys:
                 s.activation(out=d, in_=si, func=AF.Square),
                 reads=[yt_k[k]], writes=[ysq_k[k]])
        for k in range(KC):
            P.op("tensor", lambda t, k=k: t.matmul(
                s1[0:1, :], ctx.ones_bf[:], ctx.UB[:, k * 512:(k + 1) * 512],
                start=(k == 0), stop=(k == KC - 1)),
                reads=[ybf_k[k], ctx.CB], writes=s1k)
        for k in range(KC):
            P.op("tensor", lambda t, k=k: t.matmul(
                s2[0:1, :], ctx.ones_bf[:], ctx.UB[:, (16 + k) * 512:(17 + k) * 512],
                start=(k == 0), stop=(k == KC - 1)),
                reads=[ysq_k[k], ctx.CB], writes=s2k)
        P.op("vector", lambda v: v.memset(erow2[0:1, 0:1], 1e-5), writes=erow2k)
        P.op("vector", lambda v: v.tensor_scalar(
            out=rm, in0=s1[0:1, :], scalar1=1.0 / 2048, scalar2=None, op0=MUL),
            reads=s1k, writes=rmk)
        P.op("vector", lambda v: v.tensor_tensor(out=rt, in0=rm, in1=rm, op=MUL),
             reads=rmk, writes=rtk)
        P.op("vector", lambda v: v.scalar_tensor_tensor(
            out=rv, in0=s2[0:1, :], scalar=1.0 / 2048, in1=rt, op0=MUL, op1=SUB),
            reads=s2k + rtk, writes=rvk)
        P.op("scalar", lambda s: s.activation(out=rs, in_=rv, func=AF.Sqrt,
                                              bias=erow2[0:1, 0:1]),
             reads=rvk + erow2k, writes=rsk)
        P.op("vector", lambda v: v.reciprocal(rr, rs), reads=rsk, writes=rrk)
        P.op("vector", lambda v: v.tensor_tensor(out=rmr, in0=rm, in1=rr, op=MUL),
             reads=rmk + rrk, writes=rmrk)
        P.op("tensor", lambda t: t.matmul(rbc[:], ctx.onesrow_f32[:], rr,
                                          start=True, stop=True),
             reads=rrk + [ctx.CB], writes=rbck)
        P.op("tensor", lambda t: t.matmul(mbc[:], ctx.onesrow_f32[:], rmr,
                                          start=True, stop=True),
             reads=rmrk + [ctx.CB], writes=mbck)
        for k in range(KC):
            ys = yt[:, k * 512:(k + 1) * 512]
            t1, t1k = uf(ctx, 32, 1)
            t2, t2k = uf(ctx, 33, 1)
            xct, xctk = uf(ctx, 34, 1)
            szt, sztk = uf(ctx, 35, 1)
            oab, oabk = ub(ctx, 40, 1)
            P.op("vector", lambda v, ys=ys: v.tensor_tensor(
                out=t1, in0=ys, in1=rbc[:], op=MUL),
                reads=[yt_k[k]] + rbck, writes=t1k)
            P.op("vector", lambda v: v.tensor_tensor(
                out=t2, in0=t1, in1=mbc[:], op=SUB),
                reads=t1k + mbck, writes=t2k)
            gap, gk = col(ctx, "l0_lng", k)
            bap, bk_ = col(ctx, "l0_lnb", k)
            P.op("vector", lambda v, gap=gap, bap=bap: v.tensor_scalar(
                out=t1, in0=t2, scalar1=gap, scalar2=bap, op0=MUL, op1=ADD),
                reads=t2k + [gk, bk_], writes=t1k)
            P.dma("sync", xct, xc[k * 128:(k + 1) * 128, c * TC:(c + 1) * TC],
                  reads=[("l0_xc", 0)], writes=xctk)
            dap, dk = col(ctx, "l0_D", k)
            P.op("vector", lambda v, dap=dap: v.scalar_tensor_tensor(
                out=t2, in0=xct, scalar=dap, in1=t1, op0=MUL, op1=ADD),
                reads=xctk + t1k + [dk], writes=t2k)
            P.dma("sync", szt, sz0[k * 128:(k + 1) * 128, c * TC:(c + 1) * TC],
                  reads=[("l0_sz", 0)], writes=sztk)
            P.op("vector", lambda v: v.tensor_tensor(
                out=oab, in0=t2, in1=szt, op=MUL),
                reads=t2k + sztk, writes=oabk)
            P.dma("sync", ya[k * 128:(k + 1) * 128, c * TC:(c + 1) * TC], oab,
                  reads=oabk, writes=[("l0_ya", 0)])

    def epi_att(c, mt, m0, rows, bk, bkk):
        o, ok = uf(ctx, 32 + mt % 2, 1)
        P.op("vector", lambda v: v.tensor_copy(o, bk[:]), reads=bkk, writes=ok)
        P.dma("sync", att0[m0:m0 + 128, c * TC:(c + 1) * TC], o,
              reads=ok, writes=[("l0_att", 0)])
    emit_linear(ctx, "l0att", ya, ("l0_ya", 0), 2048, D,
                ctx.d["wb_w_l0_Wout_T"], ("wb_w_l0_Wout_T", 0), epi_att)

    def epi_h(c, mt, m0, rows, bka, bkak, bkb, bkbk):
        u, uk = uf(ctx, 32, 1)
        o, ok = ub(ctx, 40, 1)
        s1c, s1ck = col(ctx, "l0_w1s", mt)
        s2c, s2ck = col(ctx, "l0_w2s", mt)
        P.op("scalar", lambda s, s1c=s1c: s.activation(
            out=u, in_=bka[:], func=AF.Silu, scale=s1c),
            reads=bkak + [s1ck], writes=uk)
        P.op("vector", lambda v, s2c=s2c: v.scalar_tensor_tensor(
            out=o, in0=bkb[:], scalar=s2c, in1=u, op0=MUL, op1=MUL),
            reads=bkbk + uk + [s2ck], writes=ok)
        P.dma("sync", hh[m0:m0 + 128, c * TC:(c + 1) * TC], o,
              reads=ok, writes=[("l0_hh", 0)])
    emit_linear_dual(ctx, "l0h", xn0, ("xn0", 0), D, 4096,
                     ctx.d["wb_w_l0_w1_T"], ("wb_w_l0_w1_T", 0),
                     ctx.d["wb_w_l0_w2_T"], ("wb_w_l0_w2_T", 0), epi_h)

    def mk_epi_gelu(dst, dkey, scol_name):
        def epi(c, mt, m0, rows, bk, bkk):
            o, ok = ub(ctx, 40 + mt % 2, 1)
            sc, sck = col(ctx, scol_name, mt)
            P.op("scalar", lambda s, sc=sc: s.activation(
                out=o, in_=bk[:], func=AF.Gelu, scale=sc),
                reads=bkk + [sck], writes=ok)
            P.dma("sync", dst[m0:m0 + 128, c * TC:(c + 1) * TC], o,
                  reads=ok, writes=[(dkey, 0)])
        return epi
    emit_linear(ctx, "l0g1", hh, ("l0_hh", 0), 4096, 4096,
                ctx.d["wb_w_l0_h1_T"], ("wb_w_l0_h1_T", 0),
                mk_epi_gelu(g1, "l0_g1", "l0_h1s"))
    emit_linear(ctx, "l0g2", g1, ("l0_g1", 0), 4096, 4096,
                ctx.d["wb_w_l0_h2_T"], ("wb_w_l0_h2_T", 0),
                mk_epi_gelu(g2, "l0_g2", "l0_h2s"))

    def epi_w3(c, mt, m0, rows, bk, bkk):
        o, ok = uf(ctx, 32 + mt % 2, 1)
        sc, sck = col(ctx, "l0_w3s", mt)
        P.op("vector", lambda v, sc=sc: v.tensor_scalar(
            out=o, in0=bk[:], scalar1=sc, scalar2=None, op0=MUL),
            reads=bkk + [sck], writes=ok)
        P.dma("sync", ffn0[m0:m0 + 128, c * TC:(c + 1) * TC], o,
              reads=ok, writes=[("l0_ffn", 0)])
    emit_linear(ctx, "l0w3", g2, ("l0_g2", 0), 4096, D,
                ctx.d["wb_w_l0_w3_T"], ("wb_w_l0_w3_T", 0), epi_w3)

    x1 = ctx.scratch("x1", [D, T], F32)
    emit_add3(ctx, "l0res", x_dram, xkey, att0, ("l0_att", 0), ffn0,
              ("l0_ffn", 0), x1, ("x1", 0))
    return x1, ("x1", 0)



def emit_output(ctx, x_dram, xkey):
    P = ctx.prog
    for k in range(D // 128):
        xt, xtk = uf(ctx, 0 + (k % 2) * 8, T // 512)
        P.dma("sync", xt, x_dram[k * 128:(k + 1) * 128, :], reads=[xkey], writes=xtk)
        for c in range(T // 128):
            pb, pbk = bank(ctx, c % 4)
            o, ok = uf(ctx, 16 + (c % 4), 1)
            P.op("tensor", lambda t, c=c, pb=pb, xt=xt: t.transpose(
                pb[:, 0:128], xt[:, c * 128:(c + 1) * 128], ctx.ident_f32[:]),
                reads=xtk + [("CONST", "idf")], writes=pbk)
            P.op("vector", lambda v, o=o, pb=pb: v.tensor_copy(o[:, 0:128], pb[:, 0:128]),
                 reads=pbk, writes=ok)
            ev = P.dma("sync", ctx.d["out"][c * 128:(c + 1) * 128, k * 128:(k + 1) * 128],
                       o[:, 0:128], reads=ok, writes=[("out", 0)])
            P.finalize([ev])



PRECAST = [
    ("w_l0_Win_T", D, 4096), ("w_l0_Wx_T", 2048, 32), ("w_l0_Wdt_T", 2048, 2048),
    ("w_l0_Wout_T", 2048, D), ("w_l0_w1_T", D, 4096), ("w_l0_w2_T", D, 4096),
    ("w_l0_h1_T", 4096, 4096), ("w_l0_h2_T", 4096, 4096), ("w_l0_w3_T", 4096, D),
    ("w_l1_Wq", D, D), ("w_l1_Wkv", D, 2048), ("w_l1_Wout", D, D),
    ("w_l2_inproj_T", D, 4384), ("w_l2_outproj_T", 2048, D),
    ("w_l2_Win", D, 64), ("w_l2_Wout", 64, D),
    ("w_l3_WqT", D, D), ("w_l3_WkT", D, D), ("w_l3_WvT", D, D), ("w_l3_WoT", D, D),
    ("w_l3_WrT", D, 8), ("w_l3_W1", D, 4096), ("w_l3_W2", 4096, D),
]


def build_program(specs, taps=(), layers=(0, 1, 2, 3)):
    nc = bass.Bass("TRN2", target_bir_lowering=False)
    stack = ExitStack()
    ctx = Ctx(nc, stack, specs, taps=taps)
    with stack:
        emit_setup(ctx)
        pc = [p for p in PRECAST if any(f"l{l}" in p[0][2:4] for l in layers)]
        emit_precast_all(ctx, pc)
        xcur, xkey = ctx.d["x_in"], ("x_in", 0)
        if 0 in layers:
            xcur, xkey = emit_l0(ctx, xcur, xkey)
        if 1 in layers:
            xcur, xkey = emit_l1(ctx, xcur, xkey)
        if 2 in layers:
            xcur, xkey = emit_l2(ctx, xcur, xkey)
        if 3 in layers:
            xcur, xkey = emit_l3(ctx, xcur, xkey)
        emit_output(ctx, xcur, xkey)
        ctx.prog.emit("main")
    return nc


def kernel(**inputs):
    h = host_prep(inputs)
    specs = host_specs(h)
    nc = build_program(specs)
    x = np.asarray(inputs["L0_attn_x"], dtype=np.float32)
    in_maps = []
    for core in range(8):
        m = dict(h)
        m["x_in"] = np.ascontiguousarray(x[core].T)
        in_maps.append(m)
    res = run_bass_kernel_spmd(nc, in_maps, core_ids=list(range(8)))
    out = np.stack([r["out"] for r in res.results], axis=0)
    return out.astype(np.float32)


def set_T(newT, newTC=512):
    global T, TC, NTC
    T = newT
    TC = newTC
    NTC = T // TC


def emit_linear2src(ctx, name, srcs, Fin, Fout, dst_dram, dkey, Tlen=None,
                    out_bf16=True):
    Tlen = T if Tlen is None else Tlen
    P = ctx.prog
    KC = Fin // 128
    NS = len(srcs)
    MB = max(1, (WB_SLOTS * 1024) // (NS * KC * 128))
    mtiles = [(m0, 128) for m0 in range(0, Fout, 128)]
    nbank = 0
    for g0 in range(0, len(mtiles), MB):
        group = mtiles[g0:g0 + MB]
        gw = sum(r for _, r in group)
        for si, (sd, sk, wd, wk) in enumerate(srcs):
            for k in range(KC):
                base = (si * KC + k) * MB * 128
                P.dma("sync", ctx.WB[:, base: base + gw],
                      wd[k * 128:(k + 1) * 128, group[0][0]:group[0][0] + gw],
                      reads=[wk], writes=[("WB", si * KC + k)])
        for c in range(Tlen // TC):
            sts = []
            for si, (sd, sk, wd, wk) in enumerate(srcs):
                st, st_k = ub(ctx, si * KC, KC)
                for k in range(KC):
                    P.dma("sync", st[:, k * 512:(k + 1) * 512],
                          sd[k * 128:(k + 1) * 128, c * TC:(c + 1) * TC],
                          reads=[sk], writes=[st_k[k]])
                sts.append((st, st_k))
            off = 0
            for mi, (m0, rows) in enumerate(group):
                bk, bkk = bank(ctx, nbank % 4)
                nbank += 1
                first = True
                for si in range(NS):
                    st, st_k = sts[si]
                    for k in range(KC):
                        base = (si * KC + k) * MB * 128
                        lhs = ctx.WB[:, base + off: base + off + rows]
                        rhs = st[:, k * 512:(k + 1) * 512]
                        last = (si == NS - 1 and k == KC - 1)
                        P.op("tensor", lambda t, lhs=lhs, rhs=rhs, bk=bk,
                             rows=rows, first=first, last=last:
                             t.matmul(bk[0:rows, :], lhs, rhs,
                                      start=first, stop=last),
                             reads=[("WB", si * KC + k), st_k[k]], writes=bkk)
                        first = False
                o, ok = ub(ctx, 40 + nbank % 2, 1) if out_bf16 else uf(ctx, 32 + nbank % 2, 1)
                P.op("vector", lambda v, o=o, bk=bk, rows=rows: v.tensor_copy(
                    o[0:rows, :], bk[0:rows, :]), reads=bkk, writes=ok)
                P.dma("sync", dst_dram[m0:m0 + rows, c * TC:(c + 1) * TC],
                      o[0:rows, :], reads=ok, writes=[(dkey, 0)])
                off += rows



def emit_l1(ctx, x_dram, xkey):
    P = ctx.prog
    NKT = T // 128
    xn1 = ctx.scratch("xn1", [D, T], BF16)
    xn2 = ctx.scratch("l1_xn2", [D, T], BF16)
    q1 = ctx.scratch("l1_q", [D, T], F32)
    k1 = ctx.scratch("l1_k", [D, T], F32)
    v1 = ctx.scratch("l1_v", [D, T], BF16)
    v1T = ctx.scratch("l1_vT", [T, D], BF16)
    qn1 = ctx.scratch("l1_qn", [D, T], BF16)
    kn1 = ctx.scratch("l1_kn", [D, T], BF16)
    qns = ctx.scratch("l1_qns", [16, T], F32)
    kns = ctx.scratch("l1_kns", [16, T], F32)
    mrow_d = ctx.scratch("l1_mrow", [16, T], F32)
    ao1 = ctx.scratch("l1_ao", [D, T], BF16)
    att1 = ctx.scratch("l1_att", [D, T], F32)
    t1re = ctx.scratch("l1_t1re", [D, D], BF16)
    t1im = ctx.scratch("l1_t1im", [D, D], BF16)
    Md = ctx.scratch("l1_M", [D, D], BF16)
    ffn1 = ctx.scratch("l1_ffn", [D, T], F32)

    emit_ln(ctx, "ln1", x_dram, xkey, D, out_bf_dram=xn1, obkey="xn1")
    emit_ln(ctx, "ln1b", xn1, ("xn1", 0), D, out_bf_dram=xn2, obkey="l1_xn2",
            g="l1_ng", b="l1_nb", src_bf16=True)

    def epi_q(c, mt, m0, rows, bk, bkk):
        o, ok = uf(ctx, 32 + mt % 2, 1)
        P.op("vector", lambda v, o=o, bk=bk: v.tensor_copy(o, bk[:]),
             reads=bkk, writes=ok)
        P.dma("sync", q1[m0:m0 + 128, c * TC:(c + 1) * TC], o,
              reads=ok, writes=[("l1_q", 0)])
    emit_linear(ctx, "l1q", xn2, ("l1_xn2", 0), D, D,
                ctx.d["wb_w_l1_Wq"], ("wb_w_l1_Wq", 0), epi_q)

    def epi_kv(c, mt, m0, rows, bk, bkk):
        if m0 < D:
            o, ok = uf(ctx, 32 + mt % 2, 1)
            P.op("vector", lambda v, o=o, bk=bk: v.tensor_copy(o, bk[:]),
                 reads=bkk, writes=ok)
            P.dma("sync", k1[m0:m0 + 128, c * TC:(c + 1) * TC], o,
                  reads=ok, writes=[("l1_k", 0)])
        else:
            o, ok = ub(ctx, 40 + mt % 2, 1)
            P.op("vector", lambda v, o=o, bk=bk: v.tensor_copy(o, bk[:]),
                 reads=bkk, writes=ok)
            P.dma("sync", v1[m0 - D:m0 - D + 128, c * TC:(c + 1) * TC], o,
                  reads=ok, writes=[("l1_v", 0)])
    emit_linear(ctx, "l1kv", xn1, ("xn1", 0), D, 2 * D,
                ctx.d["wb_w_l1_Wkv"], ("wb_w_l1_Wkv", 0), epi_kv)

    for k in range(D // 128):
        vt, vtk = ub(ctx, (k % 2) * 8, T // 512)
        P.dma("sync", vt, v1[k * 128:(k + 1) * 128, :], reads=[("l1_v", 0)], writes=vtk)
        for c in range(NKT):
            pb = ctx.psb
            pbk = [("PSB", 0)]
            o, ok = ub(ctx, 16 + (c % 4), 1)
            P.op("tensor", lambda t, c=c, vt=vt, pb=pb: t.transpose(
                pb[:, 0:128], vt[:, c * 128:(c + 1) * 128], ctx.ident_bf[:]),
                reads=vtk + [("CONST", "idb")], writes=pbk)
            P.op("vector", lambda v, o=o, pb=pb: v.tensor_copy(o[:, 0:128], pb[:, 0:128]),
                 reads=pbk, writes=ok)
            P.dma("sync", v1T[c * 128:(c + 1) * 128, k * 128:(k + 1) * 128],
                  o[:, 0:128], reads=ok, writes=[("l1_vT", 0)])

    nkn_all = ctx.sbuf("l1_nkn", [64, 32], BF16)
    nsrow = ctx.sbuf("l1_nsrow", [1, 32], F32)
    for h in range(16):
        for (src_d, skey, dst_d, dstkey, scol_name, ss_d, sskey) in (
                (q1, ("l1_q", 0), qn1, ("l1_qn", 0), "l1_qs", qns, ("l1_qns", 0)),
                (k1, ("l1_k", 0), kn1, ("l1_kn", 0), "l1_ks", kns, ("l1_kns", 0))):
            qt, qtk = uf(ctx, 0, 8)
            sq, sqk = ub(ctx, 0, 8)
            P.dma("sync", qt[0:64, 0:T], src_d[h * 64:(h + 1) * 64, :],
                  reads=[skey], writes=qtk)
            P.op("scalar", lambda s, qt=qt, sq=sq: s.activation(
                out=sq[0:64, 0:T], in_=qt[0:64, 0:T], func=AF.Square),
                reads=qtk, writes=sqk)
            for c in range(NTC):
                ssb, ssbk = bank(ctx, 0)
                rsr, rsrk = rowslot(ctx, 0)
                rrr, rrrk = rowslot(ctx, 1)
                rbc2, rbc2k = bank(ctx, 1)
                P.op("tensor", lambda t, c=c, sq=sq: t.matmul(
                    ssb[0:1, :], ctx.ones_bf[0:64, :], sq[0:64, c * 512:(c + 1) * 512],
                    start=True, stop=True),
                    reads=sqk + [ctx.CB], writes=ssbk)
                P.op("scalar", lambda s: s.activation(out=rsr, in_=ssb[0:1, :],
                                                      func=AF.Sqrt),
                     reads=ssbk, writes=rsrk)
                P.op("vector", lambda v: v.reciprocal(rrr, rsr),
                     reads=rsrk, writes=rrrk)
                P.op("tensor", lambda t: t.matmul(rbc2[0:64, :],
                                                  ctx.onesrow_f32[0:1, 0:64], rrr,
                                                  start=True, stop=True),
                     reads=rrrk + [ctx.CB], writes=rbc2k)
                ob, obk = ub(ctx, 16, 1)
                scol_ap, sck = col(ctx, scol_name, 0)
                P.op("vector", lambda v, c=c, qt=qt, ob=ob, scol_ap=scol_ap:
                     v.scalar_tensor_tensor(
                    out=ob[0:64, :], in0=qt[0:64, c * 512:(c + 1) * 512],
                    scalar=scol_ap, in1=rbc2[0:64, :], op0=MUL, op1=MUL),
                    reads=qtk + [sck] + rbc2k, writes=obk)
                P.dma("sync", dst_d[h * 64:(h + 1) * 64, c * TC:(c + 1) * TC],
                      ob[0:64, :], reads=obk, writes=[dstkey])
                sqn, sqnk = ub(ctx, 17, 1)
                ss2, ss2k = bank(ctx, 2)
                ro, rok = rowslot(ctx, 2)
                P.op("scalar", lambda s, ob=ob, sqn=sqn: s.activation(
                    out=sqn[0:64, :], in_=ob[0:64, :], func=AF.Square),
                    reads=obk, writes=sqnk)
                P.op("tensor", lambda t, sqn=sqn: t.matmul(
                    ss2[0:1, :], ctx.ones_bf[0:64, :], sqn[0:64, :],
                    start=True, stop=True),
                    reads=sqnk + [ctx.CB], writes=ss2k)
                P.op("vector", lambda v: v.tensor_copy(ro, ss2[0:1, :]),
                     reads=ss2k, writes=rok)
                P.dma("sync", ss_d[h:h + 1, c * TC:(c + 1) * TC], ro,
                      reads=rok, writes=[sskey])
        nk_t, nk_tk = uf(ctx, 16, 1)
        nsq, nsqk = ub(ctx, 18, 1)
        nss, nssk = bank(ctx, 0)
        nro, nrok = rowslot(ctx, 3)
        nrr, nrrk = rowslot(ctx, 4)
        nbc, nbck = bank(ctx, 1)
        P.dma("sync", nk_t[0:64, 0:2], ctx.d["w_l1_nk"][h * 64:(h + 1) * 64, :],
              reads=[("w_l1_nk", 0)], writes=nk_tk)
        P.op("scalar", lambda s, nk_t=nk_t, nsq=nsq: s.activation(
            out=nsq[0:64, 0:2], in_=nk_t[0:64, 0:2], func=AF.Square),
            reads=nk_tk, writes=nsqk)
        P.op("tensor", lambda t, nsq=nsq: t.matmul(
            nss[0:1, 0:2], ctx.ones_bf[0:64, :], nsq[0:64, 0:2],
            start=True, stop=True), reads=nsqk + [ctx.CB], writes=nssk)
        P.op("scalar", lambda s: s.activation(out=nro[0:1, 0:2], in_=nss[0:1, 0:2],
                                              func=AF.Sqrt),
             reads=nssk, writes=nrok)
        P.op("vector", lambda v: v.reciprocal(nrr[0:1, 0:2], nro[0:1, 0:2]),
             reads=nrok, writes=nrrk)
        P.op("tensor", lambda t: t.matmul(nbc[0:64, 0:2],
                                          ctx.onesrow_f32[0:1, 0:64], nrr[0:1, 0:2],
                                          start=True, stop=True),
             reads=nrrk + [ctx.CB], writes=nbck)
        scol_ap, sck = col(ctx, "l1_ks", 0)
        P.op("vector", lambda v, h=h, nk_t=nk_t, scol_ap=scol_ap:
             v.scalar_tensor_tensor(
            out=nkn_all[0:64, 2 * h:2 * h + 2], in0=nk_t[0:64, 0:2],
            scalar=scol_ap, in1=nbc[0:64, 0:2], op0=MUL, op1=MUL),
            reads=nk_tk + [sck] + nbck, writes=[("l1_nkn", h)])
        nsq2, nsq2k = ub(ctx, 19, 1)
        nss2, nss2k = bank(ctx, 2)
        P.op("scalar", lambda s, h=h, nsq2=nsq2: s.activation(
            out=nsq2[0:64, 0:2], in_=nkn_all[0:64, 2 * h:2 * h + 2], func=AF.Square),
            reads=[("l1_nkn", h)], writes=nsq2k)
        P.op("tensor", lambda t, nsq2=nsq2: t.matmul(
            nss2[0:1, 0:2], ctx.ones_bf[0:64, :], nsq2[0:64, 0:2],
            start=True, stop=True), reads=nsq2k + [ctx.CB], writes=nss2k)
        P.op("vector", lambda v, h=h: v.tensor_copy(nsrow[0:1, 2 * h:2 * h + 2],
                                                    nss2[0:1, 0:2]),
             reads=nss2k, writes=[("l1_nsrow", h)])
    for h in range(16):
        kmx, kmxk = rowslot(ctx, 1)
        for c in range(NTC):
            kr, krk = rowslot(ctx, 0)
            rd, rdk = rowslot(ctx, 2)
            P.dma("sync", kr, kns[h:h + 1, c * TC:(c + 1) * TC],
                  reads=[("l1_kns", 0)], writes=krk)
            P.op("vector", lambda v: v.tensor_reduce(
                out=rd[0:1, 0:1], in_=kr, axis=AX.X, op=MAXOP),
                reads=krk, writes=rdk)
            if c == 0:
                P.op("vector", lambda v: v.tensor_copy(kmx[0:1, 0:1], rd[0:1, 0:1]),
                     reads=rdk, writes=kmxk)
            else:
                P.op("vector", lambda v: v.tensor_scalar(
                    out=kmx[0:1, 0:1], in0=kmx[0:1, 0:1], scalar1=rd[0:1, 0:1],
                    scalar2=None, op0=MAXOP), reads=rdk + kmxk, writes=kmxk)
        for nn in range(2):
            P.op("vector", lambda v, h=h, nn=nn: v.tensor_scalar(
                out=kmx[0:1, 0:1], in0=kmx[0:1, 0:1],
                scalar1=nsrow[0:1, 2 * h + nn:2 * h + nn + 1], scalar2=None,
                op0=MAXOP), reads=kmxk + [("l1_nsrow", h)], writes=kmxk)
        for c in range(NTC):
            qr, qrk = rowslot(ctx, 3)
            ar, ark = rowslot(ctx, 4)
            P.dma("sync", qr, qns[h:h + 1, c * TC:(c + 1) * TC],
                  reads=[("l1_qns", 0)], writes=qrk)
            P.dma("sync", ar, ctx.d["c_alibi_qrow"][h:h + 1, c * TC:(c + 1) * TC],
                  reads=[("c_alibi_qrow", 0)], writes=ark)
            P.op("vector", lambda v: v.tensor_scalar(
                out=qr, in0=qr, scalar1=kmx[0:1, 0:1], scalar2=None, op0=MUL),
                reads=qrk + kmxk, writes=qrk)
            P.op("scalar", lambda s: s.activation(out=qr, in_=qr, func=AF.Sqrt),
                 reads=qrk, writes=qrk)
            P.op("vector", lambda v: v.tensor_tensor(out=qr, in0=qr, in1=ar, op=ADD),
                 reads=qrk + ark, writes=qrk)
            P.dma("sync", mrow_d[h:h + 1, c * TC:(c + 1) * TC], qr,
                  reads=qrk, writes=[("l1_mrow", 0)])

    vaug = ctx.sbuf("shared_bf_a", [128, 2080], BF16)
    apc = ctx.sbuf("l1_apc", [128, NKT], F32)
    nvaug = ctx.sbuf("l1_nvaug", [2, 65], BF16)
    for h in range(16):
        P.op("vector", lambda v: v.memset(vaug[:], 1.0),
             writes=[("l1_vaug", kt) for kt in range(NKT)])
        for kt in range(NKT):
            P.dma("sync", vaug[:, kt * 65:kt * 65 + 64],
                  v1T[kt * 128:(kt + 1) * 128, h * 64:(h + 1) * 64],
                  reads=[("l1_vT", 0)], writes=[("l1_vaug", kt)])
        P.dma("sync", apc[:, 0:NKT], ctx.d["c_alibi_pcol"][h],
              reads=[("c_alibi_pcol", 0)], writes=[("l1_apc", 0)])
        P.op("vector", lambda v: v.memset(nvaug[0:2, :], 1.0), writes=[("l1_nvaug", 0)])
        nvt, nvtk = uf(ctx, 30, 1)
        P.dma("sync", nvt[0:2, 0:64], ctx.d["w_l1_nvT"][2 * h:2 * h + 2, :],
              reads=[("w_l1_nvT", 0)], writes=nvtk)
        P.op("vector", lambda v: v.tensor_copy(nvaug[0:2, 0:64], nvt[0:2, 0:64]),
             reads=nvtk, writes=[("l1_nvaug", 0)])
        knt, kntk = ub(ctx, 0, 8)
        P.dma("sync", knt[0:64, 0:T], kn1[h * 64:(h + 1) * 64, :],
              reads=[("l1_kn", 0)], writes=kntk)
        for qb in range(NTC):
            qnt, qntk = ub(ctx, 8, 1)
            P.dma("sync", qnt[0:64, :], qn1[h * 64:(h + 1) * 64, qb * TC:(qb + 1) * TC],
                  reads=[("l1_qn", 0)], writes=qntk)
            mr, mrk = rowslot(ctx, 2)
            mb, mbk = bank(ctx, 4)
            mbs, mbsk = uf(ctx, 28, 1)
            P.dma("sync", mr, mrow_d[h:h + 1, qb * TC:(qb + 1) * TC],
                  reads=[("l1_mrow", 0)], writes=mrk)
            P.op("tensor", lambda t, mb=mb: t.matmul(
                mb[:], ctx.onesrow_f32[:], mr, start=True, stop=True),
                reads=mrk + [ctx.CB], writes=mbk)
            P.op("vector", lambda v, mbs=mbs, mb=mb: v.tensor_copy(mbs, mb[:]),
                 reads=mbk, writes=mbsk)
            oacc, oacck = bank(ctx, 5)
            sn, snk = bank(ctx, 0)
            un, unk = uf(ctx, 29, 1)
            pn, pnk = ub(ctx, 9, 1)
            P.op("tensor", lambda t, h=h, sn=sn, qnt=qnt: t.matmul(
                sn[0:2, :], nkn_all[0:64, 2 * h:2 * h + 2], qnt[0:64, :],
                start=True, stop=True),
                reads=[("l1_nkn", h)] + qntk + [ctx.CB], writes=snk)
            P.op("vector", lambda v, un=un, sn=sn, mbs=mbs: v.tensor_tensor(
                out=un[0:2, :], in0=sn[0:2, :], in1=mbs[0:2, :], op=SUB),
                reads=snk + mbsk, writes=unk)
            ncol, ncolk = col(ctx, "l1_alibinull", h)
            P.op("scalar", lambda s, un=un, pn=pn, ncol=ncol: s.activation(
                out=pn[0:2, :], in_=un[0:2, :], func=AF.Exp, scale=8.0, bias=ncol),
                reads=unk + [ncolk], writes=pnk)
            P.op("tensor", lambda t, oacc=oacc, pn=pn: t.matmul(
                oacc[0:65, :], nvaug[0:2, :], pn[0:2, :],
                start=True, stop=False),
                reads=pnk + [("l1_nvaug", 0)], writes=oacck)
            nkt_hi = min(NKT, (qb + 1) * 4)
            for kt in range(nkt_hi):
                sb_, sbk_ = bank(ctx, 1 + kt % 2)
                ut, utk = uf(ctx, 24 + kt % 2, 1)
                pt_, ptk_ = ub(ctx, 10 + kt % 2, 1)
                rel = kt * 128 - qb * TC
                P.op("tensor", lambda t, kt=kt, sb_=sb_, knt=knt, qnt=qnt: t.matmul(
                    sb_[:], knt[0:64, kt * 128:(kt + 1) * 128], qnt[0:64, :],
                    start=True, stop=True),
                    reads=kntk + qntk, writes=sbk_)
                if rel >= 0:
                    mk = ctx.masks[rel]
                    P.op("vector", lambda v, ut=ut, sb_=sb_, mk=mk:
                         v.tensor_tensor(out=ut, in0=sb_[:], in1=mk[:], op=ADD),
                         reads=sbk_ + [("CONST", f"mask{rel}")], writes=utk)
                    P.op("vector", lambda v, ut=ut, mbs=mbs: v.tensor_tensor(
                        out=ut, in0=ut, in1=mbs, op=SUB),
                        reads=utk + mbsk, writes=utk)
                else:
                    P.op("vector", lambda v, ut=ut, sb_=sb_, mbs=mbs:
                         v.tensor_tensor(out=ut, in0=sb_[:], in1=mbs, op=SUB),
                         reads=sbk_ + mbsk, writes=utk)
                P.op("scalar", lambda s, ut=ut, pt_=pt_, kt=kt: s.activation(
                    out=pt_, in_=ut, func=AF.Exp, scale=8.0, bias=apc[:, kt:kt + 1]),
                    reads=utk + [("l1_apc", 0)], writes=ptk_)
                P.op("tensor", lambda t, kt=kt, oacc=oacc, pt_=pt_,
                     nkt_hi=nkt_hi: t.matmul(
                    oacc[0:65, :], vaug[:, kt * 65:(kt + 1) * 65], pt_[:],
                    start=False, stop=(kt == nkt_hi - 1)),
                    reads=ptk_ + [("l1_vaug", kt)], writes=oacck)
            dr, drk = rowslot(ctx, 5)
            db, dbk = bank(ctx, 0)
            dbs, dbsk = uf(ctx, 26, 1)
            ob2, ob2k = ub(ctx, 12, 1)
            P.op("vector", lambda v, dr=dr, oacc=oacc: v.reciprocal(
                dr, oacc[64:65, :]), reads=oacck, writes=drk)
            P.op("tensor", lambda t, db=db: t.matmul(
                db[0:64, :], ctx.onesrow_f32[0:1, 0:64], dr, start=True, stop=True),
                reads=drk + [ctx.CB], writes=dbk)
            P.op("vector", lambda v, dbs=dbs, db=db: v.tensor_copy(
                dbs[0:64, :], db[0:64, :]), reads=dbk, writes=dbsk)
            P.op("vector", lambda v, ob2=ob2, oacc=oacc, dbs=dbs: v.tensor_tensor(
                out=ob2[0:64, :], in0=oacc[0:64, :], in1=dbs[0:64, :], op=MUL),
                reads=oacck + dbsk, writes=ob2k)
            P.dma("sync", ao1[h * 64:(h + 1) * 64, qb * TC:(qb + 1) * TC],
                  ob2[0:64, :], reads=ob2k, writes=[("l1_ao", 0)])

    def epi_att1(c, mt, m0, rows, bk, bkk):
        o, ok = uf(ctx, 32 + mt % 2, 1)
        P.op("vector", lambda v, o=o, bk=bk: v.tensor_copy(o, bk[:]),
             reads=bkk, writes=ok)
        P.dma("sync", att1[m0:m0 + 128, c * TC:(c + 1) * TC], o,
              reads=ok, writes=[("l1_att", 0)])
    emit_linear(ctx, "l1att", ao1, ("l1_ao", 0), D, D,
                ctx.d["wb_w_l1_Wout"], ("wb_w_l1_Wout", 0), epi_att1)

    emit_linear2src(ctx, "l1t1re",
                    [(ctx.d["c_G_re"], ("c_G_re", 0), ctx.d["w_l1_wreT"], ("w_l1_wreT", 0)),
                     (ctx.d["c_G_im_neg"], ("c_G_im_neg", 0), ctx.d["w_l1_wimT"], ("w_l1_wimT", 0))],
                    D, D, t1re, "l1_t1re", Tlen=D)
    emit_linear2src(ctx, "l1t1im",
                    [(ctx.d["c_G_im"], ("c_G_im", 0), ctx.d["w_l1_wreT"], ("w_l1_wreT", 0)),
                     (ctx.d["c_G_re"], ("c_G_re", 0), ctx.d["w_l1_wimT"], ("w_l1_wimT", 0))],
                    D, D, t1im, "l1_t1im", Tlen=D)
    emit_linear2src(ctx, "l1M",
                    [(t1re, ("l1_t1re", 0), ctx.d["c_F_re"], ("c_F_re", 0)),
                     (t1im, ("l1_t1im", 0), ctx.d["c_F_im_neg"], ("c_F_im_neg", 0))],
                    D, D, Md, "l1_M", Tlen=D)

    def epi_ffn1(c, mt, m0, rows, bk, bkk):
        o, ok = uf(ctx, 32 + mt % 2, 1)
        bcol, bck = col(ctx, "l1_bias", mt)
        P.op("scalar", lambda s, o=o, bk=bk, bcol=bcol: s.activation(
            out=o, in_=bk[:], func=AF.Identity, bias=bcol),
            reads=bkk + [bck], writes=ok)
        P.dma("sync", ffn1[m0:m0 + 128, c * TC:(c + 1) * TC], o,
              reads=ok, writes=[("l1_ffn", 0)])
    emit_linear(ctx, "l1ffn", xn1, ("xn1", 0), D, D,
                Md, ("l1_M", 0), epi_ffn1)

    x2 = ctx.scratch("x2", [D, T], F32)
    emit_add3(ctx, "l1res", x_dram, xkey, att1, ("l1_att", 0), ffn1,
              ("l1_ffn", 0), x2, ("x2", 0))
    return x2, ("x2", 0)



def emit_l2(ctx, x_dram, xkey):
    P = ctx.prog
    NCH = T // 128
    xn2a = ctx.scratch("xn2a", [D, T], BF16)
    xn2f = ctx.scratch("l2_xnf", [D, T], F32)
    u2 = ctx.scratch("l2_u", [D, T], BF16)
    sz2 = ctx.scratch("l2_sz", [2048, T], F32)
    xbc2 = ctx.scratch("l2_xbc", [2304, T], F32)
    xsb = ctx.scratch("l2_xsb", [2048, T], BF16)
    bc2 = ctx.scratch("l2_bc", [256, T], BF16)
    dt2 = ctx.scratch("l2_dt", [32, T], F32)
    grows_d = ctx.scratch("l2_grows", [32, T], F32)
    ecr_d = ctx.scratch("l2_ecr", [32, T], F32)
    ehe_d = ctx.scratch("l2_ehed", [32, NCH], F32)
    a1T_d = ctx.scratch("l2_a1T", [T, 32], F32)
    wT_d = ctx.scratch("l2_wT", [T, 32], F32)
    ecT_d = ctx.scratch("l2_ecT", [T, 32], F32)
    Xtm_d = ctx.scratch("l2_Xtm", [T, 2048], BF16)
    ytm_d = ctx.scratch("l2_ytm", [T, 2048], BF16)
    yfm_d = ctx.scratch("l2_yfm", [2048, T], BF16)
    ygn_d = ctx.scratch("l2_ygn", [2048, T], BF16)
    att2 = ctx.scratch("l2_att", [D, T], F32)
    s_d = ctx.scratch("l2_s", [64, T], F32)
    gb_d = ctx.scratch("l2_gb", [64, T], BF16)
    ffn2 = ctx.scratch("l2_ffn", [D, T], F32)

    emit_ln(ctx, "ln2", x_dram, xkey, D, out_bf_dram=xn2a, obkey="xn2a",
            out_f32_dram=xn2f, ofkey="l2_xnf")
    emit_ln(ctx, "ln2b", xn2a, ("xn2a", 0), D, out_bf_dram=u2, obkey="l2_u",
            g="l2_lnw", b="l2_lnb", src_bf16=True)

    def epi_inproj(c, mt, m0, rows, bk, bkk):
        if m0 < 2048:
            o, ok = uf(ctx, 32 + mt % 2, 1)
            P.op("scalar", lambda s, o=o, bk=bk: s.activation(
                out=o, in_=bk[:], func=AF.Silu), reads=bkk, writes=ok)
            P.dma("sync", sz2[m0:m0 + 128, c * TC:(c + 1) * TC], o,
                  reads=ok, writes=[("l2_sz", 0)])
        elif m0 < 4352:
            o, ok = uf(ctx, 32 + mt % 2, 1)
            P.op("vector", lambda v, o=o, bk=bk: v.tensor_copy(o, bk[:]),
                 reads=bkk, writes=ok)
            P.dma("sync", xbc2[m0 - 2048:m0 - 2048 + 128, c * TC:(c + 1) * TC], o,
                  reads=ok, writes=[("l2_xbc", 0)])
        else:
            ua, uak = uf(ctx, 34, 1)
            e1, e1k = uf(ctx, 35, 1)
            l1_, l1k = uf(ctx, 36, 1)
            r1, r1k = uf(ctx, 37, 1)
            dtt, dttk = uf(ctx, 38, 1)
            bcol = ctx.cols["wc_l2_dtb"][0:32, 0:1]
            bck = ("CONST", "wc_l2_dtb")
            P.op("scalar", lambda s, ua=ua, bk=bk, bcol=bcol: s.activation(
                out=ua[0:32, :], in_=bk[0:32, :], func=AF.Abs, bias=bcol),
                reads=bkk + [bck], writes=uak)
            P.op("scalar", lambda s, e1=e1, ua=ua: s.activation(
                out=e1[0:32, :], in_=ua[0:32, :], func=AF.Exp, scale=-1.0),
                reads=uak, writes=e1k)
            P.op("scalar", lambda s, l1_=l1_, e1=e1: s.activation(
                out=l1_[0:32, :], in_=e1[0:32, :], func=AF.Ln,
                bias=ctx.c_one[0:32, 0:1]),
                reads=e1k + [ctx.CB], writes=l1k)
            P.op("scalar", lambda s, r1=r1, bk=bk, bcol=bcol: s.activation(
                out=r1[0:32, :], in_=bk[0:32, :], func=AF.Relu, bias=bcol),
                reads=bkk + [bck], writes=r1k)
            P.op("vector", lambda v, dtt=dtt, l1_=l1_, r1=r1: v.tensor_tensor(
                out=dtt[0:32, :], in0=l1_[0:32, :], in1=r1[0:32, :], op=ADD),
                reads=l1k + r1k, writes=dttk)
            P.dma("sync", dt2[0:32, c * TC:(c + 1) * TC], dtt[0:32, :],
                  reads=dttk, writes=[("l2_dt", 0)])
    emit_linear(ctx, "l2proj", u2, ("l2_u", 0), D, 4384,
                ctx.d["wb_w_l2_inproj_T"], ("wb_w_l2_inproj_T", 0), epi_inproj)

    for cc in range(18):
        for c in range(NTC):
            xin, xink = uf(ctx, 32, 2)
            acc_a, acc_ak = uf(ctx, 34, 1)
            acc_b, acc_bk = uf(ctx, 35, 1)
            obf, obk = ub(ctx, 40, 1)
            halo = xin[:, 0:515]
            if c == 0:
                P.op("vector", lambda v, xin=xin: v.memset(xin[:, 0:3], 0.0),
                     writes=xink)
                P.dma("sync", xin[:, 3:515], xbc2[cc * 128:(cc + 1) * 128, 0:512],
                      reads=[("l2_xbc", 0)], writes=xink)
            else:
                P.dma("sync", xin[:, 0:515],
                      xbc2[cc * 128:(cc + 1) * 128, c * TC - 3:(c + 1) * TC],
                      reads=[("l2_xbc", 0)], writes=xink)
            convcols = ctx.cols["wc_l2_conv"]
            cwk = ("CONST", "wc_l2_conv")
            P.op("vector", lambda v, cc=cc, acc_a=acc_a, halo=halo: v.tensor_scalar(
                out=acc_a, in0=halo[:, 0:512], scalar1=convcols[:, cc * 4:cc * 4 + 1],
                scalar2=None, op0=MUL), reads=xink + [cwk], writes=acc_ak)
            for kk in range(1, 4):
                src_ap = (acc_a, acc_ak) if kk % 2 == 1 else (acc_b, acc_bk)
                dst_ap = (acc_b, acc_bk) if kk % 2 == 1 else (acc_a, acc_ak)
                P.op("vector", lambda v, cc=cc, kk=kk, s_=src_ap[0], d_=dst_ap[0],
                     halo=halo: v.scalar_tensor_tensor(
                    out=d_, in0=halo[:, kk:kk + 512],
                    scalar=convcols[:, cc * 4 + kk:cc * 4 + kk + 1],
                    in1=s_, op0=MUL, op1=ADD),
                    reads=xink + src_ap[1] + [cwk], writes=dst_ap[1])
            bcol, bcolk = col(ctx, "l2_convb", cc)
            P.op("scalar", lambda s, obf=obf, acc_b=acc_b, bcol=bcol: s.activation(
                out=obf, in_=acc_b, func=AF.Silu, bias=bcol),
                reads=acc_bk + [bcolk], writes=obk)
            if cc < 16:
                P.dma("sync", xsb[cc * 128:(cc + 1) * 128, c * TC:(c + 1) * TC], obf,
                      reads=obk, writes=[("l2_xsb", 0)])
            else:
                P.dma("sync", bc2[(cc - 16) * 128:(cc - 15) * 128,
                                  c * TC:(c + 1) * TC], obf,
                      reads=obk, writes=[("l2_bc", 0)])

    dts, dtsk = uf(ctx, 0, 8)
    gsb, gsbk = uf(ctx, 8, 8)
    zeros8, zeros8k = uf(ctx, 16, 8)
    lnd, lndk = uf(ctx, 24, 8)
    wrk, wrkk = uf(ctx, 32, 8)
    Acol2 = ctx.sbuf("l2_Acol", [32, 2], F32)
    gends = ctx.sbuf("l2_gends", [32, 2 * ((T + 4095) // 4096) * 32], F32)
    ehe_sb = ctx.sbuf("l2_ehesb", [32, 64], F32)
    ehebc = ctx.sbuf("l2_ehebc", [128, 1024], F32)
    Dbc = ctx.sbuf("l2_Dbc", [128, 32], F32)

    P.dma("sync", dts[0:32, 0:T], dt2[0:32, :], reads=[("l2_dt", 0)], writes=dtsk)
    alog2 = ctx.cols["wc_l2_Alog"]
    P.op("scalar", lambda s: s.activation(out=Acol2[:, 0:1], in_=alog2[0:32, 0:1],
                                          func=AF.Exp),
         reads=[("CONST", "wc_l2_Alog")], writes=[("l2_Acol", 0)])
    P.op("vector", lambda v: v.tensor_scalar(out=Acol2[:, 1:2], in0=Acol2[:, 0:1],
                                             scalar1=-1.0, scalar2=None, op0=MUL),
         reads=[("l2_Acol", 0)], writes=[("l2_Acol", 1)])
    P.op("vector", lambda v: v.memset(zeros8[0:32, 0:T], 0.0), writes=zeros8k)
    P.op("vector", lambda v: v.tensor_scalar(
        out=wrk[0:32, 0:T], in0=dts[0:32, 0:T],
        scalar1=Acol2[:, 1:2], scalar2=None, op0=MUL),
        reads=dtsk + [("l2_Acol", 1)], writes=wrkk)
    P.op("vector", lambda v: v.tensor_tensor_scan(
        out=gsb[0:32, 0:T], data0=wrk[0:32, 0:T], data1=zeros8[0:32, 0:T],
        initial=0.0, op0=ADD, op1=ADD), reads=wrkk + zeros8k, writes=gsbk)
    P.dma("sync", grows_d[0:32, :], gsb[0:32, 0:T], reads=gsbk, writes=[("l2_grows", 0)])
    P.op("scalar", lambda s: s.activation(out=lnd[0:32, 0:T], in_=dts[0:32, 0:T],
                                          func=AF.Ln),
         reads=dtsk, writes=lndk)
    P.op("vector", lambda v: v.tensor_tensor(out=lnd[0:32, 0:T], in0=lnd[0:32, 0:T],
                                             in1=gsb[0:32, 0:T], op=SUB),
         reads=gsbk + lndk, writes=lndk)
    g3 = gsb[0:32, 0:T].rearrange("p (c q) -> p c q", q=128)
    P.op("vector", lambda v: v.tensor_copy(
        gends[0:32, 0:NCH], g3[:, :, 127:128]), reads=gsbk, writes=[("l2_gends", 0)])
    P.op("vector", lambda v: v.memset(gends[0:32, NCH:NCH + 1], 0.0),
         writes=[("l2_gends", 1)])
    P.op("vector", lambda v: v.tensor_copy(
        gends[0:32, NCH + 1:2 * NCH], gends[0:32, 0:NCH - 1]),
        reads=[("l2_gends", 0)], writes=[("l2_gends", 1)])
    a13 = lnd[0:32, 0:T].rearrange("p (c q) -> p c q", q=128)
    w3 = wrk[0:32, 0:T].rearrange("p (c q) -> p c q", q=128)
    ge3 = gends[0:32, 0:NCH].rearrange("p c -> p c 1" if False else "p (c o) -> p c o", o=1)
    o_b, g_b = bass.broadcast_tensor_aps(w3, ge3)
    P.op("vector", lambda v: v.tensor_tensor(out=o_b, in0=a13, in1=g_b, op=ADD),
         reads=lndk + [("l2_gends", 0)], writes=wrkk)
    P.op("scalar", lambda s: s.activation(out=wrk[0:32, 0:T], in_=wrk[0:32, 0:T],
                                          func=AF.Exp),
         reads=wrkk, writes=wrkk)
    for (tile, tkeys, dst, dkey) in ((lnd, lndk, a1T_d, "l2_a1T"),
                                     (wrk, wrkk, wT_d, "l2_wT")):
        for c in range(NCH):
            pb, pbk = bank(ctx, c % 2)
            o, ok = uf(ctx, 36 + c % 2, 1)
            P.op("tensor", lambda t, c=c, tile=tile, pb=pb: t.transpose(
                pb[0:128, 0:32], tile[0:32, c * 128:(c + 1) * 128],
                ctx.ident_f32[0:32, 0:32]),
                reads=tkeys + [("CONST", "idf")], writes=pbk)
            P.op("vector", lambda v, o=o, pb=pb: v.tensor_copy(
                o[:, 0:32], pb[0:128, 0:32]), reads=pbk, writes=ok)
            P.dma("sync", dst[c * 128:(c + 1) * 128, :], o[:, 0:32],
                  reads=ok, writes=[(dkey, 0)])
    gs3 = gends[0:32, NCH:2 * NCH].rearrange("p (c o) -> p c o", o=1)
    o_b2, g_b2 = bass.broadcast_tensor_aps(w3, gs3)
    P.op("vector", lambda v: v.tensor_tensor(out=o_b2, in0=g3, in1=g_b2, op=SUB),
         reads=gsbk + [("l2_gends", 1)], writes=wrkk)
    P.op("scalar", lambda s: s.activation(out=wrk[0:32, 0:T], in_=wrk[0:32, 0:T],
                                          func=AF.Exp),
         reads=wrkk, writes=wrkk)
    P.dma("sync", ecr_d[0:32, :], wrk[0:32, 0:T], reads=wrkk, writes=[("l2_ecr", 0)])
    for c in range(NCH):
        pb, pbk = bank(ctx, c % 2)
        o, ok = uf(ctx, 36 + c % 2, 1)
        P.op("tensor", lambda t, c=c, pb=pb: t.transpose(
            pb[0:128, 0:32], wrk[0:32, c * 128:(c + 1) * 128],
            ctx.ident_f32[0:32, 0:32]),
            reads=wrkk + [("CONST", "idf")], writes=pbk)
        P.op("vector", lambda v, o=o, pb=pb: v.tensor_copy(
            o[:, 0:32], pb[0:128, 0:32]), reads=pbk, writes=ok)
        P.dma("sync", ecT_d[c * 128:(c + 1) * 128, :], o[:, 0:32],
              reads=ok, writes=[("l2_ecT", 0)])
    P.op("vector", lambda v: v.tensor_tensor(
        out=ehe_sb[0:32, 0:NCH], in0=gends[0:32, 0:NCH],
        in1=gends[0:32, NCH:2 * NCH], op=SUB),
        reads=[("l2_gends", 0), ("l2_gends", 1)], writes=[("l2_ehesb", 0)])
    P.op("scalar", lambda s: s.activation(out=ehe_sb[0:32, 0:NCH],
                                          in_=ehe_sb[0:32, 0:NCH], func=AF.Exp),
         reads=[("l2_ehesb", 0)], writes=[("l2_ehesb", 0)])
    P.dma("sync", ehe_d[0:32, :], ehe_sb[0:32, 0:NCH],
          reads=[("l2_ehesb", 0)], writes=[("l2_ehed", 0)])
    ehrow = ctx.UF[0:1, 38 * 512:38 * 512 + 32 * NCH]
    ehrowk = [("UF", 38), ("UF", 39)]
    P.dma("sync", ehrow, bass.AP(ehe_d.tensor, 0, [[0, 1], [1, 32 * NCH]]),
          reads=[("l2_ehed", 0)], writes=ehrowk)
    for half in range((32 * NCH + 511) // 512):
        w_ = min(512, 32 * NCH - half * 512)
        pbx, pbxk = bank(ctx, 2)
        P.op("tensor", lambda t, half=half, w_=w_, pbx=pbx: t.matmul(
            pbx[:, 0:w_], ctx.onesrow_f32[:],
            ctx.UF[0:1, 38 * 512 + half * 512:38 * 512 + half * 512 + w_],
            start=True, stop=True),
            reads=ehrowk + [ctx.CB], writes=pbxk)
        P.op("vector", lambda v, half=half, w_=w_, pbx=pbx: v.tensor_copy(
            ehebc[:, half * 512:half * 512 + w_], pbx[:, 0:w_]),
            reads=pbxk, writes=[("l2_ehebc", half)])
    dbc_ps, dbc_psk = bank(ctx, 3)
    P.op("tensor", lambda t: t.matmul(dbc_ps[:, 0:32], ctx.onesrow_f32[:],
                                      ctx.cols["wc_l2_D"][0:1, 0:32],
                                      start=True, stop=True),
         reads=[("CONST", "wc_l2_D"), ctx.CB], writes=dbc_psk)
    P.op("vector", lambda v: v.tensor_copy(Dbc[:], dbc_ps[:, 0:32]),
         reads=dbc_psk, writes=[("l2_Dbc", 0)])

    for k in range(16):
        vt, vtk = ub(ctx, (k % 2) * 8, T // 512)
        P.dma("sync", vt, xsb[k * 128:(k + 1) * 128, :], reads=[("l2_xsb", 0)],
              writes=vtk)
        for c in range(NCH):
            pb = ctx.psb
            pbk = [("PSB", 0)]
            o, ok = ub(ctx, 16 + (c % 4), 1)
            P.op("tensor", lambda t, c=c, vt=vt, pb=pb: t.transpose(
                pb[:, 0:128], vt[:, c * 128:(c + 1) * 128], ctx.ident_bf[:]),
                reads=vtk + [("CONST", "idb")], writes=pbk)
            P.op("vector", lambda v, o=o, pb=pb: v.tensor_copy(o[:, 0:128],
                                                               pb[:, 0:128]),
                 reads=pbk, writes=ok)
            P.dma("sync", Xtm_d[c * 128:(c + 1) * 128, k * 128:(k + 1) * 128],
                  o[:, 0:128], reads=ok, writes=[("l2_Xtm", 0)])

    Hnp = ctx.sbuf("l2_Hnp", [128, 2048], F32)
    Hbf = ctx.sbuf("shared_bf_a", [128, 2080], BF16)
    mdiag = ctx.sbuf("l2_mdiag", [128, 128], F32)
    P.dma("sync", mdiag[:], ctx.d["c_maskdiag"], writes=[("CONST", "mdiag")])
    P.op("vector", lambda v: v.memset(Hnp[:], 0.0),
         writes=[("l2_Hnp", h) for h in range(32)])
    P.op("vector", lambda v: v.memset(Hbf[:, 0:2048], 0.0),
         writes=[("shared_bf_a", h) for h in range(32)])
    for c in range(NCH):
        xt_c, xt_ck = ub(ctx, 0, 4)
        Bc, Bck = ub(ctx, 4, 1)
        Cc, Cck = ub(ctx, 5, 1)
        Btm, Btmk = ub(ctx, 6, 1)
        Mr, Mrk = ub(ctx, 7, 1)
        EA, EAk = ub(ctx, 8, 8)
        PB_, PBk = ub(ctx, 16, 8)
        FA, FAk = uf(ctx, 16, 8)
        FB, FBk = uf(ctx, 24, 8)
        yst, ystk = ub(ctx, 24, 4)
        a1c, a1ck = uf(ctx, 0, 1)
        wTc, wTck = uf(ctx, 1, 1)
        ecc, ecck = uf(ctx, 2, 1)
        u1, u1k = uf(ctx, 3, 1)
        t2_, t2k = uf(ctx, 4, 1)
        xw, xwk = uf(ctx, 5, 1)
        xwb, xwbk = ub(ctx, 28, 1)
        grow_u = ctx.UF[0:1, 8 * 512:8 * 512 + 32 * 128]
        growk = [("UF", i) for i in range(8, 16)]
        P.dma("sync", xt_c, Xtm_d[c * 128:(c + 1) * 128, :],
              reads=[("l2_Xtm", 0)], writes=xt_ck)
        P.dma("sync", Bc[:, 0:128], bc2[0:128, c * 128:(c + 1) * 128],
              reads=[("l2_bc", 0)], writes=Bck)
        P.dma("sync", Cc[:, 0:128], bc2[128:256, c * 128:(c + 1) * 128],
              reads=[("l2_bc", 0)], writes=Cck)
        P.dma("sync", a1c[:, 0:32], a1T_d[c * 128:(c + 1) * 128, :],
              reads=[("l2_a1T", 0)], writes=a1ck)
        P.dma("sync", wTc[:, 0:32], wT_d[c * 128:(c + 1) * 128, :],
              reads=[("l2_wT", 0)], writes=wTck)
        P.dma("sync", ecc[:, 0:32], ecT_d[c * 128:(c + 1) * 128, :],
              reads=[("l2_ecT", 0)], writes=ecck)
        P.dma("sync", grow_u,
              bass.AP(grows_d.tensor, c * 128, [[0, 1], [T, 32], [1, 128]]),
              reads=[("l2_grows", 0)], writes=growk)
        pbb = ctx.psb
        pbbk = [("PSB", 0)]
        P.op("tensor", lambda t, Bc=Bc: t.transpose(
            pbb[:, 0:128], Bc[:, 0:128], ctx.ident_bf[:]),
            reads=Bck + [("CONST", "idb")], writes=pbbk)
        P.op("vector", lambda v, Btm=Btm: v.tensor_copy(Btm[:, 0:128], pbb[:, 0:128]),
             reads=pbbk, writes=Btmk)
        mrp, mrpk = bank(ctx, 4)
        P.op("tensor", lambda t, Bc=Bc, Cc=Cc, mrp=mrp: t.matmul(
            mrp[:, 0:128], Bc[:, 0:128], Cc[:, 0:128], start=True, stop=True),
            reads=Bck + Cck, writes=mrpk)
        P.op("vector", lambda v, Mr=Mr, mrp=mrp: v.tensor_copy(
            Mr[:, 0:128], mrp[:, 0:128]), reads=mrpk, writes=Mrk)
        for hh in range(32):
            P.op("scalar", lambda s, hh=hh: s.activation(
                out=ctx.UF[:, 16 * 512 + hh * 128: 16 * 512 + hh * 128 + 128],
                in_=mdiag[:], func=AF.Identity, bias=a1c[:, hh:hh + 1]),
                reads=[("CONST", "mdiag")] + a1ck, writes=[FAk[hh // 4]])
        for qq in range(8):
            tbp, tbpk = bank(ctx, qq % 4)
            P.op("tensor", lambda t, qq=qq, tbp=tbp: t.matmul(
                tbp[:], ctx.onesrow_f32[:],
                ctx.UF[0:1, 8 * 512 + qq * 512: 8 * 512 + (qq + 1) * 512],
                start=True, stop=True),
                reads=growk + [ctx.CB], writes=tbpk)
            P.op("vector", lambda v, qq=qq, tbp=tbp: v.tensor_tensor(
                out=ctx.UF[:, (24 + qq) * 512:(25 + qq) * 512],
                in0=ctx.UF[:, (16 + qq) * 512:(17 + qq) * 512], in1=tbp[:], op=ADD),
                reads=[FAk[qq]] + tbpk, writes=[FBk[qq]])
            P.op("scalar", lambda s, qq=qq: s.activation(
                out=ctx.UB[:, (8 + qq) * 512:(9 + qq) * 512],
                in_=ctx.UF[:, (24 + qq) * 512:(25 + qq) * 512], func=AF.Exp),
                reads=[FBk[qq]], writes=[EAk[qq]])
        e3 = ctx.UB[:, 8 * 512:16 * 512].rearrange("p (h q) -> p h q", q=128)
        p3 = ctx.UB[:, 16 * 512:24 * 512].rearrange("p (h q) -> p h q", q=128)
        m3 = Mr[:, 0:128].rearrange("p (o q) -> p o q", o=1)
        p_b, m_b = bass.broadcast_tensor_aps(p3, m3)
        P.op("vector", lambda v: v.tensor_tensor(out=p_b, in0=e3, in1=m_b, op=MUL),
             reads=EAk + Mrk, writes=PBk)
        for hh in range(32):
            yi, yik = bank(ctx, hh % 2)
            yc, yck = bank(ctx, 2 + hh % 2)
            hd, hdk = bank(ctx, 4 + hh % 2)
            Ph = ctx.UB[:, 16 * 512 + hh * 128:16 * 512 + (hh + 1) * 128]
            Xh = xt_c[:, hh * 64:(hh + 1) * 64]
            P.op("tensor", lambda t, Ph=Ph, Xh=Xh, yi=yi: t.matmul(
                yi[0:128, 0:64], Ph, Xh, start=True, stop=True),
                reads=[PBk[hh // 4]] + xt_ck, writes=yik)
            P.op("tensor", lambda t, hh=hh, Cc=Cc, yc=yc: t.matmul(
                yc[0:128, 0:64], Cc[:, 0:128], Hbf[:, hh * 64:(hh + 1) * 64],
                start=True, stop=True),
                reads=Cck + [("shared_bf_a", hh)], writes=yck)
            P.op("scalar", lambda s, hh=hh, u1=u1, yc=yc: s.activation(
                out=u1[:, 0:64], in_=yc[0:128, 0:64], func=AF.Identity,
                scale=ecc[:, hh:hh + 1]),
                reads=yck + ecck, writes=u1k)
            P.op("vector", lambda v, hh=hh, t2_=t2_, u1=u1, Xh=Xh: v.scalar_tensor_tensor(
                out=t2_[:, 0:64], in0=Xh, scalar=Dbc[:, hh:hh + 1], in1=u1[:, 0:64],
                op0=MUL, op1=ADD),
                reads=xt_ck + [("l2_Dbc", 0)] + u1k, writes=t2k)
            P.op("vector", lambda v, hh=hh, t2_=t2_, yi=yi: v.tensor_tensor(
                out=ctx.UB[:, 24 * 512 + hh * 64:24 * 512 + (hh + 1) * 64],
                in0=t2_[:, 0:64], in1=yi[0:128, 0:64], op=ADD),
                reads=t2k + yik, writes=[ystk[hh // 8]])
            P.op("vector", lambda v, hh=hh, xwb=xwb, Xh=Xh: v.tensor_scalar(
                out=xwb[:, 0:64], in0=Xh, scalar1=wTc[:, hh:hh + 1], scalar2=None,
                op0=MUL), reads=xt_ck + wTck, writes=xwbk)
            P.op("tensor", lambda t, Btm=Btm, xwb=xwb, hd=hd: t.matmul(
                hd[0:128, 0:64], Btm[:, 0:128], xwb[:, 0:64], start=True, stop=True),
                reads=Btmk + xwbk, writes=hdk)
            P.op("vector", lambda v, hh=hh, c=c, hd=hd: v.scalar_tensor_tensor(
                out=Hnp[:, hh * 64:(hh + 1) * 64], in0=Hnp[:, hh * 64:(hh + 1) * 64],
                scalar=ehebc[:, hh * NCH + c:hh * NCH + c + 1],
                in1=hd[0:128, 0:64], op0=MUL, op1=ADD),
                reads=[("l2_Hnp", hh), ("l2_ehebc", (hh * NCH + c) // 512)] + hdk,
                writes=[("l2_Hnp", hh)])
        P.op("vector", lambda v: v.tensor_copy(Hbf[:, 0:2048], Hnp[:]),
             reads=[("l2_Hnp", hh) for hh in range(32)],
             writes=[("shared_bf_a", hh) for hh in range(32)])
        for ss in range(4):
            P.dma("sync", ytm_d[c * 128:(c + 1) * 128, ss * 512:(ss + 1) * 512],
                  ctx.UB[:, (24 + ss) * 512:(25 + ss) * 512],
                  reads=[ystk[ss]], writes=[("l2_ytm", 0)])

    for c in range(NCH):
        vt, vtk = ub(ctx, (c % 2) * 4, 4)
        P.dma("sync", vt, ytm_d[c * 128:(c + 1) * 128, :],
              reads=[("l2_ytm", 0)], writes=vtk)
        for k in range(16):
            pb = ctx.psb
            pbk = [("PSB", 0)]
            o, ok = ub(ctx, 16 + (k % 4), 1)
            P.op("tensor", lambda t, k=k, vt=vt, pb=pb: t.transpose(
                pb[:, 0:128], vt[:, k * 128:(k + 1) * 128], ctx.ident_bf[:]),
                reads=vtk + [("CONST", "idb")], writes=pbk)
            P.op("vector", lambda v, o=o, pb=pb: v.tensor_copy(o[:, 0:128],
                                                               pb[:, 0:128]),
                 reads=pbk, writes=ok)
            P.dma("sync", yfm_d[k * 128:(k + 1) * 128, c * 128:(c + 1) * 128],
                  o[:, 0:128], reads=ok, writes=[("l2_yfm", 0)])

    erow3, erow3k = rowslot(ctx, 6)
    P.op("vector", lambda v: v.memset(erow3[0:1, 0:1], 1e-5), writes=erow3k)
    for c in range(NTC):
        ybf, ybf_k = ub(ctx, 0, 16)
        szt, szt_k = uf(ctx, 0, 16)
        yg, yg_k = uf(ctx, 16, 16)
        sq_, sq_k = ub(ctx, 16, 16)
        s2b, s2bk = bank(ctx, 1)
        rbc, rbck = bank(ctx, 2)
        rv, rvk = rowslot(ctx, 2)
        rs, rsk = rowslot(ctx, 3)
        rr, rrk = rowslot(ctx, 4)
        for k in range(16):
            P.dma("sync", ybf[:, k * 512:(k + 1) * 512],
                  yfm_d[k * 128:(k + 1) * 128, c * TC:(c + 1) * TC],
                  reads=[("l2_yfm", 0)], writes=[ybf_k[k]])
            P.dma("sync", szt[:, k * 512:(k + 1) * 512],
                  sz2[k * 128:(k + 1) * 128, c * TC:(c + 1) * TC],
                  reads=[("l2_sz", 0)], writes=[szt_k[k]])
            P.op("vector", lambda v, k=k, ybf=ybf, szt=szt, yg=yg: v.tensor_tensor(
                out=yg[:, k * 512:(k + 1) * 512], in0=ybf[:, k * 512:(k + 1) * 512],
                in1=szt[:, k * 512:(k + 1) * 512], op=MUL),
                reads=[ybf_k[k], szt_k[k]], writes=[yg_k[k]])
            P.op("scalar", lambda s, k=k, yg=yg, sq_=sq_: s.activation(
                out=sq_[:, k * 512:(k + 1) * 512], in_=yg[:, k * 512:(k + 1) * 512],
                func=AF.Square), reads=[yg_k[k]], writes=[sq_k[k]])
        for k in range(16):
            P.op("tensor", lambda t, k=k, sq_=sq_: t.matmul(
                s2b[0:1, :], ctx.ones_bf[:], sq_[:, k * 512:(k + 1) * 512],
                start=(k == 0), stop=(k == 15)),
                reads=[sq_k[k], ctx.CB], writes=s2bk)
        P.op("vector", lambda v: v.tensor_scalar(
            out=rv, in0=s2b[0:1, :], scalar1=1.0 / 2048, scalar2=None, op0=MUL),
            reads=s2bk, writes=rvk)
        P.op("scalar", lambda s: s.activation(out=rs, in_=rv, func=AF.Sqrt,
                                              bias=erow3[0:1, 0:1]),
             reads=rvk + erow3k, writes=rsk)
        P.op("vector", lambda v: v.reciprocal(rr, rs), reads=rsk, writes=rrk)
        P.op("tensor", lambda t: t.matmul(rbc[:], ctx.onesrow_f32[:], rr,
                                          start=True, stop=True),
             reads=rrk + [ctx.CB], writes=rbck)
        for k in range(16):
            ob, obk2 = ub(ctx, 40, 1)
            gap, gk = col(ctx, "l2_gamma", k)
            t1_, t1k_ = uf(ctx, 32, 1)
            P.op("vector", lambda v, k=k, yg=yg, t1_=t1_: v.tensor_tensor(
                out=t1_, in0=yg[:, k * 512:(k + 1) * 512], in1=rbc[:], op=MUL),
                reads=[yg_k[k]] + rbck, writes=t1k_)
            P.op("vector", lambda v, gap=gap, ob=ob, t1_=t1_: v.tensor_scalar(
                out=ob, in0=t1_, scalar1=gap, scalar2=None, op0=MUL),
                reads=t1k_ + [gk], writes=obk2)
            P.dma("sync", ygn_d[k * 128:(k + 1) * 128, c * TC:(c + 1) * TC], ob,
                  reads=obk2, writes=[("l2_ygn", 0)])

    def epi_att2(c, mt, m0, rows, bk, bkk):
        xnf, xnfk = uf(ctx, 34 + mt % 2, 1)
        o, ok = uf(ctx, 32 + mt % 2, 1)
        P.dma("sync", xnf, xn2f[m0:m0 + 128, c * TC:(c + 1) * TC],
              reads=[("l2_xnf", 0)], writes=xnfk)
        P.op("vector", lambda v, o=o, bk=bk, xnf=xnf: v.tensor_tensor(
            out=o, in0=bk[:], in1=xnf, op=ADD), reads=bkk + xnfk, writes=ok)
        P.dma("sync", att2[m0:m0 + 128, c * TC:(c + 1) * TC], o,
              reads=ok, writes=[("l2_att", 0)])
    emit_linear(ctx, "l2att", ygn_d, ("l2_ygn", 0), 2048, D,
                ctx.d["wb_w_l2_outproj_T"], ("wb_w_l2_outproj_T", 0), epi_att2)

    def epi_s(c, mt, m0, rows, bk, bkk):
        o, ok = uf(ctx, 32, 1)
        bcol = ctx.cols["wc_l2_bin"][0:64, 0:1]
        P.op("scalar", lambda s, o=o, bk=bk, bcol=bcol: s.activation(
            out=o[0:64, :], in_=bk[0:64, :], func=AF.Identity, bias=bcol),
            reads=bkk + [("CONST", "wc_l2_bin")], writes=ok)
        P.dma("sync", s_d[0:64, c * TC:(c + 1) * TC], o[0:64, :],
              reads=ok, writes=[("l2_s", 0)])
    emit_linear(ctx, "l2s", xn2a, ("xn2a", 0), D, 64,
                ctx.d["wb_w_l2_Win"], ("wb_w_l2_Win", 0), epi_s)

    ssb2, ssb2k = uf(ctx, 0, 8)
    gacc, gacck = uf(ctx, 8, 8)
    tts, ttsk = uf(ctx, 16, 8)
    rrs, rrsk = uf(ctx, 24, 8)
    gbf, gbfk = ub(ctx, 0, 8)
    P.dma("sync", ssb2[0:64, 0:T], s_d[0:64, :], reads=[("l2_s", 0)], writes=ssb2k)
    W1c = ctx.cols["wc_l2_W1"]
    b1c = ctx.cols["wc_l2_b1"]
    W2c = ctx.cols["wc_l2_W2"]
    for i in range(16):
        P.op("vector", lambda v, i=i: v.tensor_scalar(
            out=tts[0:64, 0:T], in0=ssb2[0:64, 0:T], scalar1=W1c[0:64, i:i + 1],
            scalar2=b1c[0:64, i:i + 1], op0=MUL, op1=ADD),
            reads=ssb2k + [("CONST", "wc_l2_W1"), ("CONST", "wc_l2_b1")], writes=ttsk)
        P.op("scalar", lambda s, i=i: s.activation(
            out=rrs[0:64, 0:T], in_=tts[0:64, 0:T], func=AF.Relu),
            reads=ttsk, writes=rrsk)
        if i == 0:
            P.op("vector", lambda v, i=i: v.tensor_scalar(
                out=gacc[0:64, 0:T], in0=rrs[0:64, 0:T], scalar1=W2c[0:64, i:i + 1],
                scalar2=None, op0=MUL),
                reads=rrsk + [("CONST", "wc_l2_W2")], writes=gacck)
        else:
            P.op("vector", lambda v, i=i: v.scalar_tensor_tensor(
                out=gacc[0:64, 0:T], in0=rrs[0:64, 0:T], scalar=W2c[0:64, i:i + 1],
                in1=gacc[0:64, 0:T], op0=MUL, op1=ADD),
                reads=rrsk + [("CONST", "wc_l2_W2")] + gacck, writes=gacck)
    bcol2 = ctx.cols["wc_l2_b2"][0:64, 0:1]
    P.op("vector", lambda v: v.tensor_scalar(
        out=gbf[0:64, 0:T], in0=gacc[0:64, 0:T], scalar1=1.0,
        scalar2=bcol2, op0=MUL, op1=ADD),
        reads=gacck + [("CONST", "wc_l2_b2")], writes=gbfk)
    P.dma("sync", gb_d[0:64, :], gbf[0:64, 0:T], reads=gbfk, writes=[("l2_gb", 0)])

    def epi_ffn2(c, mt, m0, rows, bk, bkk):
        o, ok = uf(ctx, 32 + mt % 2, 1)
        bcol, bck = col(ctx, "l2_bout", mt)
        P.op("scalar", lambda s, o=o, bk=bk, bcol=bcol: s.activation(
            out=o, in_=bk[:], func=AF.Identity, bias=bcol),
            reads=bkk + [bck], writes=ok)
        P.dma("sync", ffn2[m0:m0 + 128, c * TC:(c + 1) * TC], o,
              reads=ok, writes=[("l2_ffn", 0)])
    emit_linear(ctx, "l2ffn", gb_d, ("l2_gb", 0), 64, D,
                ctx.d["wb_w_l2_Wout"], ("wb_w_l2_Wout", 0), epi_ffn2)

    x3 = ctx.scratch("x3", [D, T], F32)
    emit_add3(ctx, "l2res", x_dram, xkey, att2, ("l2_att", 0), ffn2,
              ("l2_ffn", 0), x3, ("x3", 0))
    return x3, ("x3", 0)



def emit_l3(ctx, x_dram, xkey):
    P = ctx.prog
    NKT = T // 128
    xn3 = ctx.scratch("xn3", [D, T], BF16)
    qr_d = ctx.scratch("l3_qr", [512, T], BF16)
    kr_d = ctx.scratch("l3_kr", [512, T], BF16)
    v3 = ctx.scratch("l3_v", [D, T], BF16)
    v3T = ctx.scratch("l3_vT", [T, D], BF16)
    qns3 = ctx.scratch("l3_qns", [16, T], F32)
    kns3 = ctx.scratch("l3_kns", [16, T], F32)
    mrow3 = ctx.scratch("l3_mrow", [16, T], F32)
    ao3 = ctx.scratch("l3_ao", [D, T], BF16)
    att3 = ctx.scratch("l3_att", [D, T], F32)
    p7_d = ctx.scratch("l3_p7", [1, T], F32)
    h3_d = ctx.scratch("l3_h3", [4096, T], BF16)
    ffn3 = ctx.scratch("l3_ffn", [D, T], F32)

    emit_ln(ctx, "ln3", x_dram, xkey, D, out_bf_dram=xn3, obkey="xn3")

    sel32 = ctx.sbuf("l3_sel32", [64, 2], BF16)
    P.dma("sync", sel32[:], ctx.d["c_sel32bf"], writes=[("CONST", "sel32")])
    sel7 = ctx.sbuf("l3_sel7", [8, 1], F32)
    P.dma("sync", sel7[:], ctx.d["c_sel7f"], writes=[("CONST", "sel7")])

    def mk_epi_rope(dst_d, dkey, bname, ss_d, sskey):
        state = {}

        def epi(c, mt, m0, rows, bk, bkk):
            if c not in state:
                cs, csk = uf(ctx, 36, 1)
                P.dma("sync", cs[:, :], ctx.d["c_cossin"][:, c * TC:(c + 1) * TC],
                      reads=[("c_cossin", 0)], writes=csk)
                state[c] = (cs, csk)
            cs, csk = state[c]
            u, uk = uf(ctx, 32, 1)
            t1, t1k = uf(ctx, 33, 1)
            t2, t2k = uf(ctx, 34, 1)
            ob, obk = ub(ctx, 40, 1)
            sq, sqk = ub(ctx, 41, 1)
            bcol, bck = col(ctx, bname, mt)
            P.op("scalar", lambda s, u=u, bk=bk, bcol=bcol: s.activation(
                out=u, in_=bk[:], func=AF.Identity, bias=bcol),
                reads=bkk + [bck], writes=uk)
            P.op("vector", lambda v, t1=t1, u=u, cs=cs: v.tensor_tensor(
                out=t1[0:64, :], in0=u[0:64, :], in1=cs[0:64, :], op=MUL),
                reads=uk + csk, writes=t1k)
            P.op("vector", lambda v, t2=t2, u=u, cs=cs: v.tensor_tensor(
                out=t2[0:64, :], in0=u[64:128, :], in1=cs[64:128, :], op=MUL),
                reads=uk + csk, writes=t2k)
            P.op("vector", lambda v, ob=ob, t1=t1, t2=t2: v.tensor_tensor(
                out=ob[0:64, :], in0=t1[0:64, :], in1=t2[0:64, :], op=SUB),
                reads=t1k + t2k, writes=obk)
            P.dma("sync", dst_d[mt * 64:(mt + 1) * 64, c * TC:(c + 1) * TC],
                  ob[0:64, :], reads=obk, writes=[(dkey, 0)])
            ssp, sspk = bank(ctx, 4)
            ro2, rok = uf(ctx, 35, 1)
            P.op("scalar", lambda s, sq=sq, ob=ob: s.activation(
                out=sq[0:64, :], in_=ob[0:64, :], func=AF.Square),
                reads=obk, writes=sqk)
            P.op("tensor", lambda t, ssp=ssp, sq=sq: t.matmul(
                ssp[0:2, :], sel32[:], sq[0:64, :], start=True, stop=True),
                reads=sqk + [("CONST", "sel32")], writes=sspk)
            P.op("vector", lambda v, ro2=ro2, ssp=ssp: v.tensor_copy(
                ro2[0:2, :], ssp[0:2, :]), reads=sspk, writes=rok)
            P.dma("sync", ss_d[2 * mt:2 * mt + 2, c * TC:(c + 1) * TC], ro2[0:2, :],
                  reads=rok, writes=[(sskey, 0)])
        return epi

    emit_linear(ctx, "l3q", xn3, ("xn3", 0), D, D, ctx.d["wb_w_l3_WqT"],
                ("wb_w_l3_WqT", 0), mk_epi_rope(qr_d, "l3_qr", "l3_bq",
                                                qns3, "l3_qns"))
    emit_linear(ctx, "l3k", xn3, ("xn3", 0), D, D, ctx.d["wb_w_l3_WkT"],
                ("wb_w_l3_WkT", 0), mk_epi_rope(kr_d, "l3_kr", "l3_bk",
                                                kns3, "l3_kns"))

    def epi_v(c, mt, m0, rows, bk, bkk):
        o, ok = ub(ctx, 40 + mt % 2, 1)
        bcol, bck = col(ctx, "l3_bv", mt)
        P.op("scalar", lambda s, o=o, bk=bk, bcol=bcol: s.activation(
            out=o, in_=bk[:], func=AF.Identity, bias=bcol),
            reads=bkk + [bck], writes=ok)
        P.dma("sync", v3[m0:m0 + 128, c * TC:(c + 1) * TC], o,
              reads=ok, writes=[("l3_v", 0)])
    emit_linear(ctx, "l3v", xn3, ("xn3", 0), D, D, ctx.d["wb_w_l3_WvT"],
                ("wb_w_l3_WvT", 0), epi_v)

    for k in range(D // 128):
        vt, vtk = ub(ctx, (k % 2) * 8, T // 512)
        P.dma("sync", vt, v3[k * 128:(k + 1) * 128, :], reads=[("l3_v", 0)], writes=vtk)
        for c in range(NKT):
            pb = ctx.psb
            pbk = [("PSB", 0)]
            o, ok = ub(ctx, 16 + (c % 4), 1)
            P.op("tensor", lambda t, c=c, vt=vt, pb=pb: t.transpose(
                pb[:, 0:128], vt[:, c * 128:(c + 1) * 128], ctx.ident_bf[:]),
                reads=vtk + [("CONST", "idb")], writes=pbk)
            P.op("vector", lambda v, o=o, pb=pb: v.tensor_copy(o[:, 0:128],
                                                               pb[:, 0:128]),
                 reads=pbk, writes=ok)
            P.dma("sync", v3T[c * 128:(c + 1) * 128, k * 128:(k + 1) * 128],
                  o[:, 0:128], reads=ok, writes=[("l3_vT", 0)])

    for h in range(16):
        kmx, kmxk = rowslot(ctx, 1)
        for c in range(NTC):
            kr, krk = rowslot(ctx, 0)
            rd, rdk = rowslot(ctx, 2)
            P.dma("sync", kr, kns3[h:h + 1, c * TC:(c + 1) * TC],
                  reads=[("l3_kns", 0)], writes=krk)
            P.op("vector", lambda v: v.tensor_reduce(
                out=rd[0:1, 0:1], in_=kr, axis=AX.X, op=MAXOP),
                reads=krk, writes=rdk)
            if c == 0:
                P.op("vector", lambda v: v.tensor_copy(kmx[0:1, 0:1], rd[0:1, 0:1]),
                     reads=rdk, writes=kmxk)
            else:
                P.op("vector", lambda v: v.tensor_scalar(
                    out=kmx[0:1, 0:1], in0=kmx[0:1, 0:1], scalar1=rd[0:1, 0:1],
                    scalar2=None, op0=MAXOP), reads=rdk + kmxk, writes=kmxk)
        for c in range(NTC):
            qr, qrk = rowslot(ctx, 3)
            P.dma("sync", qr, qns3[h:h + 1, c * TC:(c + 1) * TC],
                  reads=[("l3_qns", 0)], writes=qrk)
            P.op("vector", lambda v: v.tensor_scalar(
                out=qr, in0=qr, scalar1=kmx[0:1, 0:1], scalar2=None, op0=MUL),
                reads=qrk + kmxk, writes=qrk)
            P.op("scalar", lambda s: s.activation(out=qr, in_=qr, func=AF.Sqrt),
                 reads=qrk, writes=qrk)
            P.dma("sync", mrow3[h:h + 1, c * TC:(c + 1) * TC], qr,
                  reads=qrk, writes=[("l3_mrow", 0)])

    vaug = ctx.sbuf("shared_bf_a", [128, 2080], BF16)
    for h in range(16):
        P.op("vector", lambda v: v.memset(vaug[:], 1.0),
             writes=[("shared_bf_a", kt) for kt in range(NKT)])
        for kt in range(NKT):
            P.dma("sync", vaug[:, kt * 65:kt * 65 + 64],
                  v3T[kt * 128:(kt + 1) * 128, h * 64:(h + 1) * 64],
                  reads=[("l3_vT", 0)], writes=[("shared_bf_a", kt)])
        qrh, qrhk = ub(ctx, 0, 8)
        krh, krhk = ub(ctx, 8, 8)
        r0 = (h // 2) * 64 + (h % 2) * 32
        P.dma("sync", qrh[0:32, 0:T], qr_d[r0:r0 + 32, :],
              reads=[("l3_qr", 0)], writes=qrhk)
        P.dma("sync", krh[0:32, 0:T], kr_d[r0:r0 + 32, :],
              reads=[("l3_kr", 0)], writes=krhk)
        if DBG_L3 and h == 0:
            dbgq = ctx.scratch("dbg_q", [32, T], BF16)
            dbgk = ctx.scratch("dbg_k", [32, T], BF16)
            P.dma("sync", dbgq[0:32, :], qrh[0:32, 0:T], reads=qrhk,
                  writes=[("dbg_q", 0)])
            P.dma("sync", dbgk[0:32, :], krh[0:32, 0:T], reads=krhk,
                  writes=[("dbg_k", 0)])
        for qb in range(NTC):
            mr, mrk = rowslot(ctx, 2)
            mb, mbk = bank(ctx, 4)
            mbs, mbsk = uf(ctx, 28, 1)
            P.dma("sync", mr, mrow3[h:h + 1, qb * TC:(qb + 1) * TC],
                  reads=[("l3_mrow", 0)], writes=mrk)
            P.op("tensor", lambda t, mb=mb: t.matmul(
                mb[:], ctx.onesrow_f32[:], mr, start=True, stop=True),
                reads=mrk + [ctx.CB], writes=mbk)
            P.op("vector", lambda v, mbs=mbs, mb=mb: v.tensor_copy(mbs, mb[:]),
                 reads=mbk, writes=mbsk)
            if DBG_L3 and h == 0 and qb == 0:
                dbgm = ctx.scratch("dbg_m", [128, TC], F32)
                P.dma("sync", dbgm[:, :], mbs, reads=mbsk, writes=[("dbg_m", 0)])
            oacc, oacck = bank(ctx, 5)
            nkt_hi = min(NKT, (qb + 1) * 4)
            for kt in range(nkt_hi):
                sb_, sbk_ = bank(ctx, 1 + kt % 2)
                ut, utk = uf(ctx, 24 + kt % 2, 1)
                pt_, ptk_ = ub(ctx, 16 + kt % 2, 1)
                rel = kt * 128 - qb * TC
                P.op("tensor", lambda t, kt=kt, qb=qb, sb_=sb_, krh=krh, qrh=qrh:
                     t.matmul(
                    sb_[:], krh[0:32, kt * 128:(kt + 1) * 128],
                    qrh[0:32, qb * 512:(qb + 1) * 512],
                    start=True, stop=True),
                    reads=krhk + qrhk, writes=sbk_)
                if DBG_L3 and h == 0 and qb == 0:
                    dbgs = ctx.scratch("dbg_s", [4 * 128, TC], F32)
                    dsv, dsvk = uf(ctx, 30, 1)
                    P.op("vector", lambda v, dsv=dsv, sb_=sb_: v.tensor_copy(
                        dsv, sb_[:]), reads=sbk_, writes=dsvk)
                    P.dma("sync", dbgs[kt * 128:(kt + 1) * 128, :], dsv,
                          reads=dsvk, writes=[("dbg_s", 0)])
                if rel >= 0:
                    mk = ctx.masks[rel]
                    P.op("vector", lambda v, ut=ut, sb_=sb_, mk=mk:
                         v.tensor_tensor(out=ut, in0=sb_[:], in1=mk[:], op=ADD),
                         reads=sbk_ + [("CONST", f"mask{rel}")], writes=utk)
                    P.op("vector", lambda v, ut=ut, mbs=mbs: v.tensor_tensor(
                        out=ut, in0=ut, in1=mbs, op=SUB),
                        reads=utk + mbsk, writes=utk)
                else:
                    P.op("vector", lambda v, ut=ut, sb_=sb_, mbs=mbs:
                         v.tensor_tensor(out=ut, in0=sb_[:], in1=mbs, op=SUB),
                         reads=sbk_ + mbsk, writes=utk)
                P.op("scalar", lambda s, ut=ut, pt_=pt_: s.activation(
                    out=pt_, in_=ut, func=AF.Exp, scale=0.125),
                    reads=utk, writes=ptk_)
                if DBG_L3 and h == 0 and qb == 0:
                    dbgp = ctx.scratch("dbg_p", [4 * 128, TC], BF16)
                    dbgu = ctx.scratch("dbg_u", [4 * 128, TC], F32)
                    P.dma("sync", dbgp[kt * 128:(kt + 1) * 128, :], pt_[:],
                          reads=ptk_, writes=[("dbg_p", 0)])
                    P.dma("sync", dbgu[kt * 128:(kt + 1) * 128, :], ut[:],
                          reads=utk, writes=[("dbg_u", 0)])
                P.op("tensor", lambda t, kt=kt, oacc=oacc, pt_=pt_,
                     nkt_hi=nkt_hi: t.matmul(
                    oacc[0:65, :], vaug[:, kt * 65:(kt + 1) * 65], pt_[:],
                    start=(kt == 0), stop=(kt == nkt_hi - 1)),
                    reads=ptk_ + [("shared_bf_a", kt)], writes=oacck)
            dr, drk = rowslot(ctx, 5)
            db, dbk = bank(ctx, 0)
            dbs, dbsk = uf(ctx, 26, 1)
            ob2, ob2k = ub(ctx, 18, 1)
            P.op("vector", lambda v, dr=dr, oacc=oacc: v.reciprocal(
                dr, oacc[64:65, :]), reads=oacck, writes=drk)
            P.op("tensor", lambda t, db=db: t.matmul(
                db[0:64, :], ctx.onesrow_f32[0:1, 0:64], dr, start=True, stop=True),
                reads=drk + [ctx.CB], writes=dbk)
            P.op("vector", lambda v, dbs=dbs, db=db: v.tensor_copy(
                dbs[0:64, :], db[0:64, :]), reads=dbk, writes=dbsk)
            P.op("vector", lambda v, ob2=ob2, oacc=oacc, dbs=dbs: v.tensor_tensor(
                out=ob2[0:64, :], in0=oacc[0:64, :], in1=dbs[0:64, :], op=MUL),
                reads=oacck + dbsk, writes=ob2k)
            P.dma("sync", ao3[h * 64:(h + 1) * 64, qb * TC:(qb + 1) * TC],
                  ob2[0:64, :], reads=ob2k, writes=[("l3_ao", 0)])

    def epi_att3(c, mt, m0, rows, bk, bkk):
        o, ok = uf(ctx, 32 + mt % 2, 1)
        bcol, bck = col(ctx, "l3_bo", mt)
        P.op("scalar", lambda s, o=o, bk=bk, bcol=bcol: s.activation(
            out=o, in_=bk[:], func=AF.Identity, bias=bcol),
            reads=bkk + [bck], writes=ok)
        P.dma("sync", att3[m0:m0 + 128, c * TC:(c + 1) * TC], o,
              reads=ok, writes=[("l3_att", 0)])
    emit_linear(ctx, "l3att", ao3, ("l3_ao", 0), D, D,
                ctx.d["wb_w_l3_WoT"], ("wb_w_l3_WoT", 0), epi_att3)

    def epi_r(c, mt, m0, rows, bk, bkk):
        lf, lfk = uf(ctx, 32, 1)
        l7p, l7pk = bank(ctx, 1)
        l7r, l7rk = rowslot(ctx, 0)
        l7b, l7bk = bank(ctx, 2)
        df, dfk = uf(ctx, 33, 1)
        ef, efk = uf(ctx, 34, 1)
        sm, smk = bank(ctx, 3)
        pr, prk = rowslot(ctx, 1)
        P.op("vector", lambda v, lf=lf, bk=bk: v.tensor_copy(lf[0:8, :], bk[0:8, :]),
             reads=bkk, writes=lfk)
        P.op("tensor", lambda t, l7p=l7p, lf=lf: t.matmul(
            l7p[0:1, :], sel7[:], lf[0:8, :], start=True, stop=True),
            reads=lfk + [("CONST", "sel7")], writes=l7pk)
        P.op("vector", lambda v, l7r=l7r, l7p=l7p: v.tensor_copy(l7r, l7p[0:1, :]),
             reads=l7pk, writes=l7rk)
        P.op("tensor", lambda t, l7b=l7b, l7r=l7r: t.matmul(
            l7b[0:8, :], ctx.onesrow_f32[0:1, 0:8], l7r, start=True, stop=True),
            reads=l7rk + [ctx.CB], writes=l7bk)
        P.op("vector", lambda v, df=df, lf=lf, l7b=l7b: v.tensor_tensor(
            out=df[0:8, :], in0=lf[0:8, :], in1=l7b[0:8, :], op=SUB),
            reads=lfk + l7bk, writes=dfk)
        P.op("scalar", lambda s, ef=ef, df=df: s.activation(
            out=ef[0:8, :], in_=df[0:8, :], func=AF.Exp),
            reads=dfk, writes=efk)
        P.op("tensor", lambda t, sm=sm, ef=ef: t.matmul(
            sm[0:1, :], ctx.ones_f32[0:8, :], ef[0:8, :], start=True, stop=True),
            reads=efk + [ctx.CB], writes=smk)
        P.op("vector", lambda v, pr=pr, sm=sm: v.reciprocal(pr, sm[0:1, :]),
             reads=smk, writes=prk)
        P.dma("sync", p7_d[0:1, c * TC:(c + 1) * TC], pr,
              reads=prk, writes=[("l3_p7", 0)])
    emit_linear(ctx, "l3r", xn3, ("xn3", 0), D, 8,
                ctx.d["wb_w_l3_WrT"], ("wb_w_l3_WrT", 0), epi_r)

    def epi_h3(c, mt, m0, rows, bk, bkk):
        o, ok = ub(ctx, 40 + mt % 2, 1)
        bcol, bck = col(ctx, "l3_b1", mt)
        P.op("scalar", lambda s, o=o, bk=bk, bcol=bcol: s.activation(
            out=o, in_=bk[:], func=AF.Gelu, bias=bcol),
            reads=bkk + [bck], writes=ok)
        P.dma("sync", h3_d[m0:m0 + 128, c * TC:(c + 1) * TC], o,
              reads=ok, writes=[("l3_h3", 0)])
    emit_linear(ctx, "l3h", xn3, ("xn3", 0), D, 4096,
                ctx.d["wb_w_l3_W1"], ("wb_w_l3_W1", 0), epi_h3)

    def epi_ffn3(c, mt, m0, rows, bk, bkk):
        o, ok = uf(ctx, 32 + mt % 2, 1)
        pr, prk = rowslot(ctx, 2)
        pb7, pb7k = bank(ctx, 4)
        pbs, pbsk = uf(ctx, 34 + mt % 2, 1)
        bcol, bck = col(ctx, "l3_b2", mt)
        P.op("scalar", lambda s, o=o, bk=bk, bcol=bcol: s.activation(
            out=o, in_=bk[:], func=AF.Identity, bias=bcol),
            reads=bkk + [bck], writes=ok)
        P.dma("sync", pr, p7_d[0:1, c * TC:(c + 1) * TC],
              reads=[("l3_p7", 0)], writes=prk)
        P.op("tensor", lambda t, pb7=pb7, pr=pr: t.matmul(
            pb7[:], ctx.onesrow_f32[:], pr, start=True, stop=True),
            reads=prk + [ctx.CB], writes=pb7k)
        P.op("vector", lambda v, pbs=pbs, pb7=pb7: v.tensor_copy(pbs, pb7[:]),
             reads=pb7k, writes=pbsk)
        P.op("vector", lambda v, o=o, pbs=pbs: v.tensor_tensor(
            out=o, in0=o, in1=pbs, op=MUL), reads=ok + pbsk, writes=ok)
        P.dma("sync", ffn3[m0:m0 + 128, c * TC:(c + 1) * TC], o,
              reads=ok, writes=[("l3_ffn", 0)])
    emit_linear(ctx, "l3ffn", h3_d, ("l3_h3", 0), 4096, D,
                ctx.d["wb_w_l3_W2"], ("wb_w_l3_W2", 0), epi_ffn3)

    x4 = ctx.scratch("x4", [D, T], F32)
    emit_add3(ctx, "l3res", x_dram, xkey, att3, ("l3_att", 0), ffn3,
              ("l3_ffn", 0), x4, ("x4", 0))
    return x4, ("x4", 0)
